# version 34
# baseline (speedup 1.0000x reference)
import numpy as np
from contextlib import ExitStack

import concourse.bass as bass
import concourse.mybir as mybir
from concourse.bass_utils import run_bass_kernel_spmd
from concourse.alu_op_type import AluOpType

F32 = mybir.dt.float32
BF16 = mybir.dt.bfloat16
I32 = mybir.dt.int32
AF = mybir.ActivationFunctionType
AL = AluOpType

NB = 2
NT = 16
D = 1024
HID = 3072
NHT = 24
EPS = 1e-5
EPS3 = 1e-5 * 16129.0 ** 2


class Sched:

    ENGINES = ("tensor", "scalar", "vector", "gpsimd", "sync")

    def __init__(self, nc, es):
        self.nc = nc
        self.es = es
        self.cnt = {}
        self.h = {}

    def sem(self, name):
        self.h[name] = self.es.enter_context(self.nc.semaphore(name))
        self.cnt[name] = 0
        return name

    def begin(self):
        self.ops = {e: [] for e in self.ENGINES}

    def add(self, eng, fn, waits=(), inc=None, amt=1):
        w = [(s, v) for (s, v) in waits if v > 0]
        self.ops[eng].append((w, fn, inc, amt))
        if inc is not None:
            self.cnt[inc] += amt
            return self.cnt[inc]
        return None

    def flush(self, name=None):
        nc, h = self.nc, self.h
        self.nflush = getattr(self, "nflush", 0) + 1
        if getattr(self, "stop_after", None) is not None and self.nflush > self.stop_after:
            self.begin()
            return
        ops = self.ops
        with nc.Block(name) as block:
            for eng_name in self.ENGINES:
                lst = ops[eng_name]
                if not lst:
                    continue

                def body(eng, lst=lst):
                    for w, fn, inc, amt in lst:
                        for s, v in w:
                            eng.wait_ge(h[s], v)
                        if fn is None:
                            continue
                        instr = fn(eng)
                        if inc is not None:
                            instr.then_inc(h[inc], amt)

                getattr(block, eng_name)(body)
        self.begin()


def build(stop_after=None, debug_dump=False):
    nc = bass.Bass("TRN2", target_bir_lowering=False, dynamic_dma_scratch_size=16384)
    x_hbm = nc.dram_tensor("x", [NB, 2048, D], F32, kind="ExternalInput").ap()
    mq_hbm = nc.dram_tensor("mq", [D, 512], F32, kind="ExternalInput").ap()
    mk_hbm = nc.dram_tensor("mk", [D, 512], F32, kind="ExternalInput").ap()
    mv_hbm = nc.dram_tensor("mv", [D, D], F32, kind="ExternalInput").ap()
    wg_hbm = nc.dram_tensor("wg", [HID, D], I32, kind="ExternalInput").ap()
    wu_hbm = nc.dram_tensor("wu", [HID, D], I32, kind="ExternalInput").ap()
    wd_hbm = nc.dram_tensor("wd", [D, HID], I32, kind="ExternalInput").ap()
    out_hbm = nc.dram_tensor("out", [NB, 2048, D], F32, kind="ExternalOutput").ap()
    wgT_dram = nc.dram_tensor("wgT_scratch", [NHT, 128, 1024], BF16, kind="Internal").ap()
    wuT_dram = nc.dram_tensor("wuT_scratch", [NHT, 128, 1024], BF16, kind="Internal").ap()
    wdT_dram = nc.dram_tensor("wdT_scratch", [NHT, 128, 1024], BF16, kind="Internal").ap()

    es = ExitStack()
    S = Sched(nc, es)
    S.stop_after = stop_after

    def sb(name, shape, dt, scope=es):
        return scope.enter_context(nc.sbuf_tensor(name, shape, dt))

    def pt(name, shape, dt, scope=es):
        return scope.enter_context(nc.psum_tensor(name, shape, dt))

    q_x = [S.sem(f"q_x{i}") for i in range(4)]
    q_st = [S.sem(f"q_st{i}") for i in range(4)]
    q_wq = S.sem("q_wq")
    q_wk = S.sem("q_wk")
    q_wks = S.sem("q_wks")
    q_wv = S.sem("q_wv")
    q_wv1 = S.sem("q_wv1")
    q_ksw = S.sem("q_ksw")
    q_wg = [S.sem(f"q_wg{i}") for i in range(2)]
    q_wu = [S.sem(f"q_wu{i}") for i in range(2)]
    q_wst = [S.sem(f"q_wst{i}") for i in range(4)]
    q_wstd = [S.sem(f"q_wstd{i}") for i in range(2)]
    q_wgw = [S.sem(f"q_wgw{i}") for i in range(2)]
    q_wuw = [S.sem(f"q_wuw{i}") for i in range(2)]
    q_wtr = [S.sem(f"q_wtr{i}") for i in range(2)]
    q_wgt = [S.sem(f"q_wgt{i}") for i in range(2)]
    q_wut = [S.sem(f"q_wut{i}") for i in range(2)]
    q_wgi = [S.sem(f"q_wgi{i}") for i in range(2)]
    q_wui = [S.sem(f"q_wui{i}") for i in range(2)]
    q_wdi = [S.sem(f"q_wdi{i}") for i in range(2)]
    q_wdw = [S.sem(f"q_wdw{i}") for i in range(4)]
    q_wdT = S.sem("q_wdT")
    c_act = S.sem("c_act")
    c_dve = S.sem("c_dve")
    c_pool = S.sem("c_pool")
    s_xn = S.sem("s_xn")
    s_tp = S.sem("s_tp")
    s_evT = S.sem("s_evT")
    s_evD = S.sem("s_evD")
    s_proj = S.sem("s_proj")
    s_evP = S.sem("s_evP")
    s_evPA = S.sem("s_evPA")
    s_mm = S.sem("s_mm")
    s_exp = S.sem("s_exp")
    s_scale = S.sem("s_scale")
    s_av = S.sem("s_av")
    s_apply = S.sem("s_apply")
    s_xadd = S.sem("s_xadd")
    s_wT = S.sem("s_wT")
    s_wgev = S.sem("s_wgev")
    s_wuev = S.sem("s_wuev")
    s_wdev = S.sem("s_wdev")
    s_gup = S.sem("s_gup")
    s_silu = S.sem("s_silu")
    s_hT = S.sem("s_hT")
    s_down = S.sem("s_down")
    s_ap3 = S.sem("s_ap3")
    s_xadd3 = S.sem("s_xadd3")
    s_xadd3d = S.sem("s_xadd3d")
    s_xaddd = S.sem("s_xaddd")

    S.begin()

    ident = sb("ident", [128, 128], BF16)
    setup_es = ExitStack()
    io_a = sb("io_a", [128, 128], F32, setup_es)
    io_b = sb("io_b", [128, 128], F32, setup_es)
    S.add("gpsimd", lambda g: g.iota(io_a[:, :], [[1, 128]], channel_multiplier=0,
                                     allow_small_or_imprecise_dtypes=True), inc=c_pool)
    S.add("gpsimd", lambda g: g.iota(io_b[:, :], [[0, 128]], channel_multiplier=1,
                                     allow_small_or_imprecise_dtypes=True), inc=c_pool)
    S.add("vector", lambda v: v.tensor_tensor(ident[:, :], io_a[:, :], io_b[:, :], AL.is_equal),
          waits=[(c_pool, 2)], inc=c_dve)
    S.flush("setup")
    setup_es.close()

    x_sb = sb("x_sb", [128, NT * D], F32)
    xn_stage = sb("xn_stage", [128, 2 * D], BF16)
    bn1 = sb("bn1", [128, 12 * NT], F32)
    mvs = sb("mvs", [128, 2 * NT], F32)
    vfix1 = sb("vfix1", [128, NT], F32)
    rstd1 = sb("rstd1", [128, NT], F32)
    nmr1 = sb("nmr1", [128, NT], F32)
    accx = sb("accx", [128, 16], F32)
    ssum = sb("ssum", [128, 8], F32)
    rsx = sb("rsx", [128, 8], F32)
    bn2 = sb("bn2", [128, 24], F32)
    mv2 = sb("mv2", [128, 4], F32)
    vfix2 = sb("vfix2", [128, 2], F32)
    rstd2 = sb("rstd2", [128, 2], F32)
    nmr2 = sb("nmr2", [128, 2], F32)

    xld_counts = {0: {}, 1: {}}
    b1_stats_last = {}
    a2_add_bases = {}

    a1_stats_last = {}

    def hoist_a1_stats(bb, g4):
        last = None
        for t in range(4 * g4, 4 * g4 + 4):
            S.add("vector", lambda v, t=t: v.bn_stats(
                bn1[:, 12 * t:12 * t + 6], x_sb[:, D * t:D * t + 512]),
                waits=[(q_x[t // 4], xld_counts[bb][t // 4])])
            last = S.add("vector", lambda v, t=t: v.bn_stats(
                bn1[:, 12 * t + 6:12 * t + 12], x_sb[:, D * t + 512:D * t + 1024]),
                inc=c_dve)
        for t in range(4 * g4, 4 * g4 + 4):
            last = S.add("vector", lambda v, t=t: v.bn_aggr(
                mvs[:, 2 * t:2 * t + 2], bn1[:, 12 * t:12 * t + 12]),
                waits=[(c_dve, last)] if t == 4 * g4 else [], inc=c_dve)
        a1_stats_last[(bb, g4)] = last

    def hoist_b1_stats(bb, HH, hf):
        xab, xdb = a2_add_bases[bb]
        last = None
        for j in range(4 * hf, 4 * hf + 4):
            t = 8 * HH + j
            xw = (s_xadd, xab + t + 1) if t <= 13 else (s_xaddd, xdb + t - 13)
            S.add("vector", lambda v, j=j, t=t: v.bn_stats(
                bn1[:, 12 * j:12 * j + 6], x_sb[:, D * t:D * t + 512]),
                waits=[xw])
            last = S.add("vector", lambda v, j=j, t=t: v.bn_stats(
                bn1[:, 12 * j + 6:12 * j + 12], x_sb[:, D * t + 512:D * t + 1024]),
                inc=c_dve)
        for j in range(4 * hf, 4 * hf + 4):
            last = S.add("vector", lambda v, j=j: v.bn_aggr(
                mvs[:, 2 * j:2 * j + 2], bn1[:, 12 * j:12 * j + 12]),
                waits=[(c_dve, last)] if j == 4 * hf else [], inc=c_dve)
        b1_stats_last[(bb, HH, hf)] = last

    def emit_xload(bb, i):
        dst = bass.AP(x_sb, 4096 * i, [[NT * D, 128], [D, 4], [1, D]])
        srcx = bass.AP(x_hbm.tensor, bb * 2048 * D + 512 * i * D,
                       [[D, 128], [128 * D, 4], [1, D]])
        w = [(q_st[i], S.cnt[q_st[i]])] if bb > 0 else []
        return S.add("sync", lambda s, dst=dst, srcx=srcx: s.dma_start(dst, srcx),
                     waits=w, inc=q_x[i], amt=16)

    for b in range(NB):
        bs = ExitStack()

        qT = sb(f"qT_{b}", [128, 4 * 2048], BF16, bs)
        kT = sb(f"kT_{b}", [128, 4 * 2048], BF16, bs)
        kTs = sb(f"kTs_{b}", [128, 4 * 2048], BF16, bs)
        v_sb = sb(f"v_{b}", [128, NT * D], BF16, bs)

        a1 = ExitStack()
        xn_bf = xn_stage
        xnT = sb(f"xnT_{b}", [128, 8 * 2048], BF16, a1)
        w_pg = [sb(f"wpg{i}_{b}", [128, 8 * 512], BF16, a1) for i in range(2)]
        wks_sb = sb(f"wks_{b}", [128, 8 * 512], BF16, a1)
        ps_tp = [pt(f"pstp{i}_{b}", [128, 1024], BF16, a1) for i in range(2)]
        psP = [pt(f"psP{i}_{b}", [128, 512], F32, a1) for i in range(4)]

        if b == 1:
            for e in ("sync", "gpsimd", "vector", "scalar"):
                S.add(e, None, waits=[(q_wdT, S.cnt[q_wdT])])
        if b == 0:
            for i in range(4):
                xld_counts[0][i] = emit_xload(0, i)
        xld = xld_counts[b]
        dst_q = bass.AP(w_pg[0], 0, [[4096, 128], [512, 8], [1, 512]])
        src_q = bass.AP(mq_hbm.tensor, 0, [[512, 128], [65536, 8], [1, 512]])
        wq_ld = S.add("gpsimd", lambda g: g.dma_start(dst_q, src_q), inc=q_wq, amt=16)
        dst_k = bass.AP(w_pg[1], 0, [[4096, 128], [512, 8], [1, 512]])
        src_k = bass.AP(mk_hbm.tensor, 0, [[512, 128], [65536, 8], [1, 512]])
        wk_ld = S.add("gpsimd", lambda g: g.dma_start(dst_k, src_k), inc=q_wk, amt=16)
        dstv1 = bass.AP(kTs, 0, [[8192, 128], [1024, 4], [1, 1024]])
        srcv1 = bass.AP(mv_hbm.tensor, 4 * 128 * D,
                        [[D, 128], [128 * D, 4], [1, D]])
        wv1_early = S.add("gpsimd", lambda g, dstv1=dstv1, srcv1=srcv1: g.dma_start(
            dstv1, srcv1), inc=q_wv1, amt=16)

        tp_base = S.cnt[s_tp]
        evT_base = S.cnt[s_evT]
        xn_base = S.cnt[s_xn]
        for g4 in range(4):
            tiles = [4 * g4 + j for j in range(4)]
            if (b, g4) in a1_stats_last:
                aggr_done = a1_stats_last[(b, g4)]
            else:
                for t in tiles:
                    S.add("vector", lambda v, t=t: v.bn_stats(bn1[:, 12 * t:12 * t + 6],
                                                              x_sb[:, D * t:D * t + 512]),
                          waits=[(q_x[t // 4], xld[t // 4])])
                    stats_done = S.add("vector", lambda v, t=t: v.bn_stats(
                        bn1[:, 12 * t + 6:12 * t + 12], x_sb[:, D * t + 512:D * t + 1024]),
                        inc=c_dve)
                for t in tiles:
                    aggr_done = S.add("vector", lambda v, t=t: v.bn_aggr(
                        mvs[:, 2 * t:2 * t + 2], bn1[:, 12 * t:12 * t + 12]),
                        waits=[(c_dve, stats_done)] if t == tiles[0] else [], inc=c_dve)
            var_v4 = bass.AP(mvs, 8 * g4 + 1, [[2 * NT, 128], [2, 4]])
            mean_v4 = bass.AP(mvs, 8 * g4, [[2 * NT, 128], [2, 4]])
            veps_done = S.add("vector", lambda v, g4=g4, var_v4=var_v4: v.tensor_scalar(
                vfix1[:, 4 * g4:4 * g4 + 4], var_v4, EPS, None, AL.add),
                waits=[(c_dve, aggr_done)], inc=c_dve)
            ln_done = S.add("scalar", lambda sc, g4=g4: sc.activation(
                rstd1[:, 4 * g4:4 * g4 + 4], vfix1[:, 4 * g4:4 * g4 + 4], AF.Ln),
                waits=[(c_dve, veps_done)], inc=c_act)
            rstd_done = S.add("scalar", lambda sc, g4=g4: sc.activation(
                rstd1[:, 4 * g4:4 * g4 + 4], rstd1[:, 4 * g4:4 * g4 + 4], AF.Exp, scale=-0.5),
                waits=[(c_act, ln_done)], inc=c_act)
            negm_done = S.add("vector", lambda v, g4=g4, mean_v4=mean_v4: v.tensor_scalar(
                nmr1[:, 4 * g4:4 * g4 + 4], mean_v4, -1.0, None, AL.mult),
                waits=[(c_act, rstd_done)], inc=c_dve)
            for t in tiles:
                S.add("vector", lambda v, t=t: v.tensor_scalar(
                    xn_bf[:, D * (t % 2):D * (t % 2 + 1)], x_sb[:, D * t:D * (t + 1)],
                    nmr1[:, t:t + 1], rstd1[:, t:t + 1], AL.add, AL.mult),
                    waits=[(c_dve, negm_done), (s_tp, tp_base + t - 1)], inc=s_xn)
                for dt in range(8):
                    S.add("tensor", lambda tt, t=t, dt=dt: tt.transpose(
                        ps_tp[t % 2][:, 128 * dt:128 * (dt + 1)],
                        xn_bf[:, D * (t % 2) + 128 * dt:D * (t % 2) + 128 * (dt + 1)],
                        ident[:, :]),
                        waits=([(s_xn, xn_base + t + 1), (s_evT, evT_base + t - 1)]
                               if dt == 0 else []),
                        inc=(s_tp if dt == 7 else None))
                dest = bass.AP(xnT, 128 * t, [[8 * 2048, 128], [2048, 8], [1, 128]])
                S.add("scalar", lambda sc, t=t, dest=dest: sc.copy(dest, ps_tp[t % 2][:, :]),
                      waits=[(s_tp, tp_base + t + 1)], inc=s_evT)

        proj_groups = []
        for q in range(4):
            for c in range(4):
                proj_groups.append(("q", q, c))
        for q in range(4):
            for c in range(4):
                proj_groups.append(("k", q, c))
        for t in range(NT):
            for c2 in range(2):
                proj_groups.append(("v", t, c2))

        wv_ld = None
        proj_base = S.cnt[s_proj]
        evP_base = S.cnt[s_evP]
        evPA_base = S.cnt[s_evPA]

        if b == 0:
            wd_ld = {}
            wdev0 = S.cnt[s_wdev]
            wdtp0 = S.cnt[s_wT]
            wd_emitted = [0]

            def wd_stage_dma(u):
                d2t, grp = u // 3, u % 3
                w = [(s_tp, tp_base + NT)] if u < 2 else [(s_wT, wdtp0 + u - 1)]
                wd_ld[u] = S.add("gpsimd", lambda g, d2t=d2t, grp=grp, u=u: g.dma_start(
                    xn_stage[:, 1024 * (u % 2):1024 * (u % 2 + 1)],
                    wd_hbm[128 * d2t:128 * (d2t + 1), 1024 * grp:1024 * (grp + 1)]),
                    waits=w, inc=q_wdi[u % 2], amt=16)

            def wd_unit(u):
                d2t, grp = u // 3, u % 3
                sl = u % 2
                for j in range(8):
                    w = []
                    if j == 0:
                        w = [(q_wdi[sl], wd_ld[u]), (s_evT, evT_base + NT)]
                        if u >= 2:
                            w.append((s_wdev, wdev0 + u - 1))
                    S.add("tensor", lambda tt, sl=sl, j=j: tt.transpose(
                        ps_tp[sl][:, 128 * j:128 * (j + 1)],
                        xn_stage[:, 1024 * sl + 128 * j:1024 * sl + 128 * (j + 1)],
                        ident[:, :]),
                        waits=w, inc=(s_wT if j == 7 else None))
                sl4 = u % 4
                evw = [(s_wT, wdtp0 + u + 1)]
                if u >= 4:
                    evw.append((q_wdw[sl4], 16 * ((u - 4) // 4 + 1)))
                ev = S.add("vector", lambda v, sl4=sl4: v.tensor_copy(
                    wks_sb[:, 1024 * sl4:1024 * (sl4 + 1)], ps_tp[sl][:, :]),
                    waits=evw, inc=s_wdev)
                dstw = bass.AP(wdT_dram.tensor, (8 * grp) * 128 * 1024 + 128 * d2t,
                               [[1024, 128], [128 * 1024, 8], [1, 128]])
                S.add("sync", lambda s, dstw=dstw, sl4=sl4: s.dma_start(
                    dstw, wks_sb[:, 1024 * sl4:1024 * (sl4 + 1)]),
                    waits=[(s_wdev, ev)], inc=q_wdw[sl4], amt=16)

            def wd_tick(frac):
                want = min(24, int(frac * 24))
                if want > 0 and wd_emitted[0] == 0:
                    wd_stage_dma(0)
                    wd_stage_dma(1)
                while wd_emitted[0] < want:
                    u = wd_emitted[0]
                    if u + 2 < 24:
                        wd_stage_dma(u + 2)
                    wd_unit(u)
                    wd_emitted[0] += 1
        else:
            def wd_tick(frac):
                pass

        for gi, (kind, i1, i2) in enumerate(proj_groups):
            if kind == "k" and i1 == 0 and i2 == 0:
                dstv = bass.AP(w_pg[0], 0, [[4096, 128], [1024, 4], [1, 1024]])
                srcv = bass.AP(mv_hbm.tensor, 0, [[D, 128], [128 * D, 4], [1, D]])
                wv_lds = [S.add(
                    "gpsimd", lambda g, dstv=dstv, srcv=srcv: g.dma_start(dstv, srcv),
                    waits=[(s_proj, proj_base + 16)], inc=q_wv, amt=16),
                    wv1_early]

            slot = gi % 4
            for dt in range(8):
                if kind == "q":
                    lhsT = w_pg[0][:, 512 * dt + 128 * i1:512 * dt + 128 * (i1 + 1)]
                    rhs = xnT[:, 2048 * dt + 512 * i2:2048 * dt + 512 * (i2 + 1)]
                    wwait = (q_wq, wq_ld)
                elif kind == "k":
                    lhsT = w_pg[1][:, 512 * dt + 128 * i1:512 * dt + 128 * (i1 + 1)]
                    rhs = xnT[:, 2048 * dt + 512 * i2:2048 * dt + 512 * (i2 + 1)]
                    wwait = (q_wk, wk_ld)
                else:
                    lhsT = xnT[:, 2048 * dt + 128 * i1:2048 * dt + 128 * (i1 + 1)]
                    wslot = w_pg[0] if dt < 4 else kTs
                    col = 1024 * (dt % 4) + 512 * i2
                    rhs = wslot[:, col:col + 512]
                    wwait = (q_wv, wv_lds[0])
                waits = []
                if kind == "v" and dt == 4:
                    waits = [(q_wv1, wv_lds[1])]
                if dt == 0:
                    if kind == "v":
                        evt_need = evT_base + i1 + 1
                    else:
                        evt_need = evT_base + 4 * i2 + 4
                    waits = [wwait, (s_evT, evt_need)]
                    if gi >= 4:
                        if gi % 2 == 0:
                            waits.append((s_evP, evP_base + gi // 2 - 1))
                        else:
                            waits.append((s_evPA, evPA_base + (gi - 3) // 2))
                S.add("tensor", lambda tt, lhsT=lhsT, rhs=rhs, slot=slot, dt=dt: tt.matmul(
                    psP[slot][:, :], lhsT, rhs, start=(dt == 0), stop=(dt == 7)),
                    waits=waits, inc=(s_proj if dt == 7 else None))
            if kind == "q":
                dst = qT[:, 2048 * i1 + 512 * i2: 2048 * i1 + 512 * (i2 + 1)]
            elif kind == "k":
                dst = kT[:, 2048 * i1 + 512 * i2: 2048 * i1 + 512 * (i2 + 1)]
            else:
                dst = v_sb[:, D * i1 + 512 * i2: D * i1 + 512 * (i2 + 1)]
            if gi % 2 == 0:
                S.add("vector", lambda v, dst=dst, slot=slot: v.tensor_copy(dst, psP[slot][:, :]),
                      waits=[(s_proj, proj_base + gi + 1)], inc=s_evP)
            else:
                S.add("scalar", lambda sc, dst=dst, slot=slot: sc.copy(dst, psP[slot][:, :]),
                      waits=[(s_proj, proj_base + gi + 1)], inc=s_evPA)
            if kind == "k" and i2 == 3 and i1 >= 2:
                kw = [(s_evP, evP_base + 10 + 2 * i1), (s_evPA, evPA_base + 10 + 2 * i1)]
                S.add("sync", lambda s, i1=i1: s.dma_start(
                    kTs[0:64, 2048 * i1:2048 * (i1 + 1)],
                    kT[64:128, 2048 * i1:2048 * (i1 + 1)]),
                    waits=kw, inc=q_ksw, amt=16)
                S.add("sync", lambda s, i1=i1: s.dma_start(
                    kTs[64:128, 2048 * i1:2048 * (i1 + 1)],
                    kT[0:64, 2048 * i1:2048 * (i1 + 1)]),
                    inc=q_ksw, amt=16)
            if gi >= 32:
                wd_tick((gi - 31) / 32.0)
        wd_tick(1.0)
        for i1 in range(2):
            S.add("sync", lambda s, i1=i1: s.dma_start(
                kTs[0:64, 2048 * i1:2048 * (i1 + 1)],
                kT[64:128, 2048 * i1:2048 * (i1 + 1)]),
                waits=[(s_proj, proj_base + 64)] if i1 == 0 else [], inc=q_ksw, amt=16)
            S.add("sync", lambda s, i1=i1: s.dma_start(
                kTs[64:128, 2048 * i1:2048 * (i1 + 1)],
                kT[0:64, 2048 * i1:2048 * (i1 + 1)]),
                inc=q_ksw, amt=16)
        if debug_dump and b == 0:
            S.add("gpsimd", lambda g: g.dma_start(out_hbm[1, 0:128, :], x_sb[:, 0:D]),
                  waits=[(s_evP, S.cnt[s_evP]), (s_evPA, S.cnt[s_evPA])], inc=q_st[0], amt=16)
            S.add("gpsimd", lambda g: g.dma_start(out_hbm[1, 128:256, :], xnT[:, 0:D]),
                  inc=q_st[0], amt=16)
            S.add("gpsimd", lambda g: g.dma_start(out_hbm[1, 256:384, :], qT[:, 0:D]),
                  inc=q_st[0], amt=16)
            S.add("gpsimd", lambda g: g.dma_start(out_hbm[1, 384:512, :], kT[:, 0:D]),
                  inc=q_st[0], amt=16)
            S.add("gpsimd", lambda g: g.dma_start(out_hbm[1, 512:640, :], kTs[:, 0:D]),
                  inc=q_st[0], amt=16)
            S.add("gpsimd", lambda g: g.dma_start(out_hbm[1, 640:768, :], v_sb[:, 0:D]),
                  inc=q_st[0], amt=16)
            S.add("sync", None, waits=[(q_st[0], S.cnt[q_st[0]])])
        S.flush(f"a1_{b}")
        a1.close()

        if b == 0:
            for e in ("scalar", "vector", "gpsimd"):
                S.add(e, None, waits=[(q_wdw[i], S.cnt[q_wdw[i]]) for i in range(4)])
        a2 = ExitStack()
        att = sb(f"att_{b}", [128, 8 * 2048], BF16, a2)
        u_st = sb(f"u_{b}", [128, 2 * 2048], BF16, a2)
        Sp = [pt(f"S{i}_{b}", [128, 1024], F32, a2) for i in range(2)]
        Op = [pt(f"O{i}_{b}", [128, 1024], F32, a2) for i in range(2)]

        mm_base = S.cnt[s_mm]
        exp_base = S.cnt[s_exp]
        scale_base = S.cnt[s_scale]
        av_base = S.cnt[s_av]
        apply_base = S.cnt[s_apply]
        xadd_base = S.cnt[s_xadd]
        a2_add_bases[b] = (xadd_base, S.cnt[s_xaddd])

        def emit_av_step(g, t):
            h = t // 2
            sl6 = (16 * g + t) % 8
            for m in range(8):
                for dlt in range(2):
                    lhsT = att[:, 2048 * sl6 + 256 * m + 128 * dlt:
                               2048 * sl6 + 256 * m + 128 * (dlt + 1)]
                    rhs = v_sb[:, D * (2 * m + t % 2) + 128 * h:
                               D * (2 * m + t % 2) + 128 * (h + 1)]
                    waits = []
                    if m == 0 and dlt == 0:
                        waits = [(s_scale, scale_base + 16 * g + t + 1)]
                        if t == 0:
                            waits.append((s_apply, apply_base + 2 * g))
                    inc = s_av if (m == 7 and dlt == 1) else None
                    S.add("tensor", lambda tt, lhsT=lhsT, rhs=rhs, dlt=dlt, m=m, t=t: tt.matmul(
                        Op[dlt][:, 128 * m:128 * (m + 1)], lhsT, rhs,
                        start=(t == 0 and m % 4 == 0), stop=(t == 15 and m % 4 == 3)),
                        waits=waits, inc=inc)

        pending_rstd = None
        pending_dve_tail = None

        def emit_rstd(pr):
            l2 = S.add("scalar", lambda sc: sc.activation(rstd2[:, :], vfix2[:, :], AF.Ln),
                       waits=[(c_dve, pr[0])], inc=c_act)
            e2 = S.add("scalar", lambda sc: sc.activation(rstd2[:, :], rstd2[:, :],
                                                          AF.Exp, scale=-0.5),
                       waits=[(c_act, l2)], inc=c_act)
            assert e2 == pr[1], (e2, pr)

        for g in range(8):
            for k in range(32):
                if g > 0 and k in (0, 2, 4, 6, 8):
                    emit_av_step(g - 1, 11 + k // 2)
                K = 32 * g + k
                t, c = k // 2, k % 2
                hh = t // 2
                for mi in range(4):
                    m = 2 * mi + c
                    if (hh % 2) == (m % 2):
                        ksrc, kbase = kT, 64 * (hh % 2)
                    else:
                        ksrc, kbase = kTs, 64 * (1 - hh % 2)
                    lhsT = ksrc[kbase:kbase + 64,
                                2048 * (hh // 2) + 256 * g + 128 * (t % 2):
                                2048 * (hh // 2) + 256 * g + 128 * (t % 2) + 128]
                    rhs = qT[64 * (m % 2):64 * (m % 2) + 64,
                             2048 * (m // 2) + 256 * g:2048 * (m // 2) + 256 * g + 256]
                    waits = []
                    if mi == 0:
                        waits = [(s_exp, exp_base + K - 1)]
                        if g == 0 and k == 1:
                            waits.append((q_ksw, S.cnt[q_ksw]))
                    S.add("tensor", lambda tt, lhsT=lhsT, rhs=rhs, mi=mi, K=K: tt.matmul(
                        Sp[K % 2][:, 256 * mi:256 * (mi + 1)], lhsT, rhs,
                        start=True, stop=True),
                        waits=waits, inc=(s_mm if mi == 3 else None))
                if k >= 15 and k % 2 == 1:
                    emit_av_step(g, (k - 15) // 2)
            emit_av_step(g, 9)
            emit_av_step(g, 10)
            if g == 7:
                for tav in (11, 12, 13, 14, 15):
                    emit_av_step(g, tav)

            def emit_tile_chain(t, g=g):
                sl6 = (16 * g + t) % 8
                w1 = S.add("vector", lambda v, sl6=sl6: v.tensor_tensor(
                    ssum[:, sl6:sl6 + 1], accx[:, 2 * sl6:2 * sl6 + 1],
                    accx[:, 2 * sl6 + 1:2 * sl6 + 2], AL.add),
                    waits=[(s_exp, exp_base + 32 * g + 2 * t + 2)], inc=c_dve)
                w2 = S.add("vector", lambda v, sl6=sl6: v.reciprocal(
                    rsx[:, sl6:sl6 + 1], ssum[:, sl6:sl6 + 1]),
                    waits=[(c_dve, w1)], inc=c_dve)
                S.add("vector", lambda v, sl6=sl6: v.tensor_scalar(
                    att[:, 2048 * sl6:2048 * (sl6 + 1)], att[:, 2048 * sl6:2048 * (sl6 + 1)],
                    rsx[:, sl6:sl6 + 1], None, AL.mult),
                    waits=[(c_dve, w2)], inc=s_scale)

            if g >= 6:
                hoist_b1_stats(b, 0, g - 6)
            for t in range(3):
                emit_tile_chain(t)
            if pending_dve_tail is not None:
                pending_rstd = pending_dve_tail()
                pending_dve_tail = None
            for t in range(3, 16):
                emit_tile_chain(t)

            def make_tail(g=g):
                def tail():
                    avdone = av_base + 16 * g + 16
                    for dlt in range(2):
                        S.add("vector", lambda v, dlt=dlt: v.bn_stats(
                            bn2[:, 12 * dlt:12 * dlt + 6], Op[dlt][:, 0:512]),
                            waits=[(s_av, avdone)] if dlt == 0 else [])
                        last = S.add("vector", lambda v, dlt=dlt: v.bn_stats(
                            bn2[:, 12 * dlt + 6:12 * dlt + 12], Op[dlt][:, 512:1024]),
                            inc=c_dve)
                    for dlt in range(2):
                        last = S.add("vector", lambda v, dlt=dlt: v.bn_aggr(
                            mv2[:, 2 * dlt:2 * dlt + 2], bn2[:, 12 * dlt:12 * dlt + 12]),
                            waits=[(c_dve, last)] if dlt == 0 else [], inc=c_dve)
                    var2 = bass.AP(mv2, 1, [[4, 128], [2, 2]])
                    mean2 = bass.AP(mv2, 0, [[4, 128], [2, 2]])
                    v2d = S.add("vector", lambda v: v.tensor_scalar(
                        vfix2[:, :], var2, EPS, None, AL.add),
                        waits=[(c_dve, last)], inc=c_dve)
                    negm_done = S.add("vector", lambda v: v.tensor_scalar(
                        nmr2[:, :], mean2, -1.0, None, AL.mult),
                        waits=[(c_dve, v2d)], inc=c_dve)
                    e2_pred = S.cnt[c_act] + 2
                    for dlt in range(2):
                        S.add("vector", lambda v, dlt=dlt, g=g: v.tensor_scalar(
                            u_st[:, 2048 * (g % 2) + 1024 * dlt:2048 * (g % 2) + 1024 * (dlt + 1)],
                            Op[dlt][:, :], nmr2[:, dlt:dlt + 1], rstd2[:, dlt:dlt + 1],
                            AL.add, AL.mult),
                            waits=[(c_act, e2_pred), (c_dve, negm_done),
                                   (s_xadd, xadd_base + 2 * (g - 2) + dlt + 1)],
                            inc=s_apply)
                    return (v2d, e2_pred)
                return tail

            pending_dve_tail = make_tail()

            for k in range(32):
                if k == 10 and pending_rstd is not None:
                    emit_rstd(pending_rstd)
                    pending_rstd = None
                K = 32 * g + k
                t, c = k // 2, k % 2
                sl6 = (16 * g + t) % 8
                att_out = bass.AP(att, 2048 * sl6 + 256 * c,
                                  [[8 * 2048, 128], [512, 4], [1, 256]])
                S.add("scalar", lambda sc, K=K, att_out=att_out, sl6=sl6, c=c: sc.activation(
                    att_out,
                    Sp[K % 2][:, :], AF.Exp, scale=0.125,
                    accum_out=accx[:, 2 * sl6 + c:2 * sl6 + c + 1]),
                    waits=[(s_mm, mm_base + K + 1),
                           (s_av, av_base + 16 * g + t - 7)],
                    inc=s_exp)

            for dlt in range(2):
                if g == 7:
                    continue
                ttile = 2 * g + dlt
                S.add("gpsimd", lambda p, ttile=ttile, dlt=dlt, g=g: p.tensor_tensor(
                    x_sb[:, D * ttile:D * (ttile + 1)], x_sb[:, D * ttile:D * (ttile + 1)],
                    u_st[:, 2048 * (g % 2) + 1024 * dlt:2048 * (g % 2) + 1024 * (dlt + 1)],
                    AL.add),
                    waits=[(s_apply, apply_base + 2 * g + dlt + 1)], inc=s_xadd)
        pending_rstd = pending_dve_tail()
        pending_dve_tail = None
        emit_rstd(pending_rstd)
        pending_rstd = None
        for dlt in range(2):
            ttile = 14 + dlt
            S.add("vector", lambda v, ttile=ttile, dlt=dlt: v.tensor_tensor(
                x_sb[:, D * ttile:D * (ttile + 1)], x_sb[:, D * ttile:D * (ttile + 1)],
                u_st[:, 2048 + 1024 * dlt:2048 + 1024 * (dlt + 1)],
                AL.add),
                waits=[(s_apply, apply_base + 14 + dlt + 1)], inc=s_xaddd)
        if debug_dump and b == 0:
            S.add("gpsimd", lambda g: g.dma_start(out_hbm[1, 768:896, :], x_sb[:, 0:D]),
                  waits=[(s_xadd, S.cnt[s_xadd])], inc=q_st[1], amt=16)
            S.add("gpsimd", lambda g: g.dma_start(out_hbm[1, 896:1024, :], u_st[:, 0:D]),
                  inc=q_st[1], amt=16)
            S.add("gpsimd", lambda g: g.dma_start(out_hbm[1, 1024:1152, :], att[:, 0:D]),
                  inc=q_st[1], amt=16)
            S.add("gpsimd", lambda g: g.dma_start(out_hbm[1, 1152:1280, :], att[:, 2048:2048 + D]),
                  inc=q_st[1], amt=16)
            S.add("sync", None, waits=[(q_st[1], S.cnt[q_st[1]])])
        S.flush(f"a2_{b}")
        a2.close()
        bs.close()

        fb = ExitStack()
        wdT = sb(f"wdT_{b}", [128, NHT * 1024], BF16, fb)

        for H in range(2):
            fh = ExitStack()
            xn2_bf = xn_stage
            hT = sb(f"hT_{b}_{H}", [128, NHT * 1024], BF16, fh)
            wds = ExitStack()

            inline_tp = (b == 0 and H == 0)
            p1 = ExitStack()
            x2T = sb(f"x2T_{b}_{H}", [128, 8 * 1024], BF16, p1)
            stripeW = D if inline_tp else 2 * D
            wgT_s = [sb(f"wgT{i}_{b}_{H}", [128, stripeW], BF16, p1) for i in range(2)]
            wuT_s = [sb(f"wuT{i}_{b}_{H}", [128, stripeW], BF16, p1) for i in range(2)]
            if inline_tp:
                wg_st = [sb(f"wgst{i}_{b}", [128, D], BF16, p1) for i in range(2)]
                wu_st = [sb(f"wust{i}_{b}", [128, D], BF16, p1) for i in range(2)]
                psB = [pt(f"psB{i}_{b}", [128, 1024], BF16, p1) for i in range(4)]
                ps_t2 = psB[:2]
                psTw = psB[:2]
                psTu = psB[2:]
            else:
                ps_t2 = [pt(f"pst2{i}_{b}_{H}", [128, 1024], BF16, p1) for i in range(2)]
            NPS = 2 if inline_tp else 3
            psG = [pt(f"psG{i}_{b}_{H}", [128, 512], F32, p1) for i in range(NPS)]
            psU = [pt(f"psU{i}_{b}_{H}", [128, 512], F32, p1) for i in range(NPS)]

            toks = [8 * H + j for j in range(8)]
            stats_last = {}

            def emit_bstats(hf):
                last = None
                for j in range(4 * hf, 4 * hf + 4):
                    t = toks[j]
                    S.add("vector", lambda v, j=j, t=t: v.bn_stats(
                        bn1[:, 12 * j:12 * j + 6], x_sb[:, D * t:D * t + 512]))
                    last = S.add("vector", lambda v, j=j, t=t: v.bn_stats(
                        bn1[:, 12 * j + 6:12 * j + 12], x_sb[:, D * t + 512:D * t + 1024]),
                        inc=c_dve)
                for j in range(4 * hf, 4 * hf + 4):
                    last = S.add("vector", lambda v, j=j: v.bn_aggr(
                        mvs[:, 2 * j:2 * j + 2], bn1[:, 12 * j:12 * j + 12]),
                        waits=[(c_dve, last)] if j == 4 * hf else [], inc=c_dve)
                stats_last[hf] = last

            nm_done = {}

            def emit_brstd(hf):
                var_v = bass.AP(mvs, 8 * hf + 1, [[2 * NT, 128], [2, 4]])
                mean_v = bass.AP(mvs, 8 * hf, [[2 * NT, 128], [2, 4]])
                c4 = slice(4 * hf, 4 * hf + 4)
                vd = S.add("vector", lambda v, c4=c4, var_v=var_v: v.tensor_scalar(
                    vfix1[:, c4], var_v, EPS, None, AL.add),
                    waits=[(c_dve, stats_last[hf])], inc=c_dve)
                ld2 = S.add("scalar", lambda sc, c4=c4: sc.activation(
                    rstd1[:, c4], vfix1[:, c4], AF.Ln),
                    waits=[(c_dve, vd)], inc=c_act)
                ed2 = S.add("scalar", lambda sc, c4=c4: sc.activation(
                    rstd1[:, c4], rstd1[:, c4], AF.Exp, scale=-0.5),
                    waits=[(c_act, ld2)], inc=c_act)
                nm1 = S.add("vector", lambda v, c4=c4, mean_v=mean_v: v.tensor_tensor(
                    nmr1[:, c4], mean_v, rstd1[:, c4], AL.mult),
                    waits=[(c_act, ed2)], inc=c_dve)
                nm_done[hf] = S.add("vector", lambda v, c4=c4: v.tensor_scalar(
                    nmr1[:, c4], nmr1[:, c4], -1.0, None, AL.mult),
                    waits=[(c_dve, nm1)], inc=c_dve)

            tpb = S.cnt[s_tp]
            xnb = S.cnt[s_xn]
            ev_mark = {-2: None, -1: None}

            def emit_bapply(j):
                t = toks[j]
                S.add("scalar", lambda sc, j=j, t=t: sc.activation(
                    xn2_bf[:, D * (j % 2):D * (j % 2 + 1)], x_sb[:, D * t:D * (t + 1)],
                    AF.Identity, bias=nmr1[:, j:j + 1], scale=rstd1[:, j:j + 1]),
                    waits=[(c_dve, nm_done[j // 4]), (s_tp, tpb + j - 1)], inc=s_xn)
                for dt in range(8):
                    w = []
                    if dt == 0:
                        w = [(s_xn, xnb + j + 1)]
                        if ev_mark[j - 2] is not None:
                            w.append(ev_mark[j - 2])
                    S.add("tensor", lambda tt, j=j, dt=dt: tt.transpose(
                        ps_t2[j % 2][:, 128 * dt:128 * (dt + 1)],
                        xn2_bf[:, D * (j % 2) + 128 * dt:D * (j % 2) + 128 * (dt + 1)],
                        ident[:, :]),
                        waits=w, inc=(s_tp if dt == 7 else None))

            def emit_bevac(j):
                dest = bass.AP(x2T, 128 * j, [[8 * 1024, 128], [1024, 8], [1, 128]])
                if j % 2 == 0:
                    c = S.add("scalar", lambda sc, dest=dest, j=j: sc.copy(
                        dest, ps_t2[j % 2][:, :]),
                        waits=[(s_tp, tpb + j + 1)], inc=s_evT)
                    ev_mark[j] = (s_evT, c)
                else:
                    c = S.add("vector", lambda v, dest=dest, j=j: v.tensor_copy(
                        dest, ps_t2[j % 2][:, :]),
                        waits=[(s_tp, tpb + j + 1)], inc=s_evD)
                    ev_mark[j] = (s_evD, c)

            stats_last[0] = b1_stats_last[(b, H, 0)]
            stats_last[1] = b1_stats_last[(b, H, 1)]
            emit_brstd(0)
            for j in range(4):
                emit_bapply(j)
                if j > 0:
                    emit_bevac(j - 1)
            emit_brstd(1)
            for j in range(4, 8):
                emit_bapply(j)
                emit_bevac(j - 1)
            emit_bevac(7)
            x2T_c0 = [ev_mark[2], ev_mark[3]]
            x2T_w = [ev_mark[6], ev_mark[7]]

            gup_base = S.cnt[s_gup]
            silu_base = S.cnt[s_silu]
            hT_base = S.cnt[s_hT]
            ldT_g = {}
            ldT_u = {}

            if inline_tp:
                wT_base = S.cnt[s_wT]
                wgev_base = S.cnt[s_wgev]
                wuev_base = S.cnt[s_wuev]
                wgw_base = [S.cnt[q_wgw[0]], S.cnt[q_wgw[1]]]
                wuw_base = [S.cnt[q_wuw[0]], S.cnt[q_wuw[1]]]

                def emit_wdma(ht):
                    sl = ht % 2
                    ldT_g[ht] = S.add("gpsimd", lambda g, ht=ht, sl=sl: g.dma_start(
                        wg_st[sl][:, :], wg_hbm[128 * ht:128 * (ht + 1), :]),
                        waits=[(s_wT, wT_base + 2 * (ht - 2) + 1)], inc=q_wgi[sl], amt=16)
                    ldT_u[ht] = S.add("gpsimd", lambda g, ht=ht, sl=sl: g.dma_start(
                        wu_st[sl][:, :], wu_hbm[128 * ht:128 * (ht + 1), :]),
                        waits=[(s_wT, wT_base + 2 * (ht - 2) + 2)], inc=q_wui[sl], amt=16)

                def emit_wtp(ht):
                    sl = ht % 2
                    for dt in range(8):
                        S.add("tensor", lambda tt, sl=sl, dt=dt: tt.transpose(
                            psTw[sl][:, 128 * dt:128 * (dt + 1)],
                            wg_st[sl][:, 128 * dt:128 * (dt + 1)], ident[:, :]),
                            waits=([(q_wgi[sl], ldT_g[ht]), (s_wgev, wgev_base + (ht - 2) + 1)]
                                   + x2T_w if dt == 0 else []),
                            inc=(s_wT if dt == 7 else None))
                    for dt in range(8):
                        S.add("tensor", lambda tt, sl=sl, dt=dt: tt.transpose(
                            psTu[sl][:, 128 * dt:128 * (dt + 1)],
                            wu_st[sl][:, 128 * dt:128 * (dt + 1)], ident[:, :]),
                            waits=([(q_wui[sl], ldT_u[ht]), (s_wuev, wuev_base + (ht - 2) + 1)]
                                   if dt == 0 else []),
                            inc=(s_wT if dt == 7 else None))

                def emit_wev(ht):
                    sl = ht % 2
                    evg = S.add("scalar", lambda sc, sl=sl: sc.copy(wgT_s[sl][:, :], psTw[sl][:, :]),
                                waits=[(s_wT, wT_base + 2 * ht + 1),
                                       (q_wgw[sl], wgw_base[sl] + 16 * ((ht - 2) // 2 + 1)
                                        if ht >= 2 else 0)],
                                inc=s_wgev)
                    S.add("sync", lambda s, ht=ht, sl=sl: s.dma_start(
                        wgT_dram[ht, :, :], wgT_s[sl][:, :]),
                        waits=[(s_wgev, evg)], inc=q_wgw[sl], amt=16)
                    evu = S.add("vector", lambda v, sl=sl: v.tensor_copy(wuT_s[sl][:, :], psTu[sl][:, :]),
                                waits=[(s_wT, wT_base + 2 * ht + 2),
                                       (q_wuw[sl], wuw_base[sl] + 16 * ((ht - 2) // 2 + 1)
                                        if ht >= 2 else 0)],
                                inc=s_wuev)
                    S.add("sync", lambda s, ht=ht, sl=sl: s.dma_start(
                        wuT_dram[ht, :, :], wuT_s[sl][:, :]),
                        waits=[(s_wuev, evu)], inc=q_wuw[sl], amt=16)

                emit_wdma(0)
                emit_wdma(1)
                emit_wtp(0)
                emit_wev(0)
                emit_wtp(1)
                emit_wev(1)
            else:
                def emit_pair_load(p):
                    sl = p % 2
                    wgr = [(s_gup, max(gup_base + 8 * p - 9, 0))]
                    wur = [(s_gup, max(gup_base + 8 * p - 8, 0))]
                    if p < 2 and b == 0:
                        wgr += [(q_wgw[0], S.cnt[q_wgw[0]]), (q_wgw[1], S.cnt[q_wgw[1]])]
                        wur += [(q_wuw[0], S.cnt[q_wuw[0]]), (q_wuw[1], S.cnt[q_wuw[1]])]
                    srcg = bass.AP(wgT_dram.tensor, 2 * p * 128 * 1024,
                                   [[1024, 128], [128 * 1024, 2], [1, 1024]])
                    srcu = bass.AP(wuT_dram.tensor, 2 * p * 128 * 1024,
                                   [[1024, 128], [128 * 1024, 2], [1, 1024]])
                    ldT_g[p] = S.add("sync", lambda s, sl=sl, srcg=srcg: s.dma_start(
                        wgT_s[sl][:, :], srcg),
                        waits=wgr, inc=q_wgt[sl], amt=16)
                    ldT_u[p] = S.add("sync", lambda s, sl=sl, srcu=srcu: s.dma_start(
                        wuT_s[sl][:, :], srcu),
                        waits=wur, inc=q_wut[sl], amt=16)

                emit_pair_load(0)
                emit_pair_load(1)
            if H == 0:
                wdld = bass.AP(wdT_dram.tensor, 0,
                               [[1024, 128], [128 * 1024, NHT], [1, 1024]])
                wdst = bass.AP(wdT, 0, [[NHT * 1024, 128], [1024, NHT], [1, 1024]])
                S.add("sync", lambda s, wdst=wdst, wdld=wdld: s.dma_start(wdst, wdld),
                      waits=[(q_wdw[i], S.cnt[q_wdw[i]]) for i in range(4)],
                      inc=q_wdT, amt=16)
            for ht in range(NHT):
                sl = ht % 2
                if inline_tp:
                    wsl, wcol = sl, 0
                    if ht + 2 < NHT:
                        emit_wdma(ht + 2)
                else:
                    wsl, wcol = (ht // 2) % 2, D * (ht % 2)
                    if ht % 2 == 0 and ht // 2 + 2 < NHT // 2:
                        emit_pair_load(ht // 2 + 2)
                for c2 in range(2):
                    ci = 2 * ht + c2
                    gsl = ci % NPS
                    for dt in range(8):
                        waits = []
                        if dt == 0:
                            waits = ([(s_wgev, wgev_base + ht + 1)] if inline_tp
                                     else [(q_wgt[wsl], ldT_g[ht // 2])]) + [
                                     (s_silu, silu_base + ci - NPS + 1)] + (
                                     x2T_c0 if c2 == 0 else x2T_w)
                        S.add("tensor", lambda tt, wsl=wsl, wcol=wcol, dt=dt, gsl=gsl, c2=c2: tt.matmul(
                            psG[gsl][:, :], wgT_s[wsl][:, wcol + 128 * dt:wcol + 128 * (dt + 1)],
                            x2T[:, 1024 * dt + 512 * c2:1024 * dt + 512 * (c2 + 1)],
                            start=(dt == 0), stop=(dt == 7)),
                            waits=waits, inc=(s_gup if dt == 7 else None))
                    for dt in range(8):
                        waits = []
                        if dt == 0:
                            waits = ([(s_wuev, wuev_base + ht + 1)] if inline_tp
                                     else [(q_wut[wsl], ldT_u[ht // 2])]) + [
                                     (s_hT, hT_base + ci - NPS + 1)]
                        S.add("tensor", lambda tt, wsl=wsl, wcol=wcol, dt=dt, gsl=gsl, c2=c2: tt.matmul(
                            psU[gsl][:, :], wuT_s[wsl][:, wcol + 128 * dt:wcol + 128 * (dt + 1)],
                            x2T[:, 1024 * dt + 512 * c2:1024 * dt + 512 * (c2 + 1)],
                            start=(dt == 0), stop=(dt == 7)),
                            waits=waits, inc=(s_gup if dt == 7 else None))
                for c2 in range(2):
                    ci = 2 * ht + c2
                    gsl = ci % NPS
                    dstH = hT[:, 1024 * ht + 512 * c2:1024 * ht + 512 * (c2 + 1)]
                    S.add("scalar", lambda sc, gsl=gsl, dstH=dstH: sc.activation(
                        dstH, psG[gsl][:, :], AF.Silu, scale=1.0 / 127.0),
                        waits=[(s_gup, gup_base + 4 * ht + 2 * c2 + 1)], inc=s_silu)
                    S.add("vector", lambda v, gsl=gsl, dstH=dstH: v.tensor_tensor(
                        dstH, dstH, psU[gsl][:, :], AL.mult),
                        waits=[(s_silu, silu_base + ci + 1),
                               (s_gup, gup_base + 4 * ht + 2 * c2 + 2)], inc=s_hT)
                if inline_tp and ht + 2 < NHT:
                    emit_wtp(ht + 2)
                    emit_wev(ht + 2)
            S.flush(f"b1_{b}_{H}")
            p1.close()

            p2 = ExitStack()
            u2 = sb(f"u2_{b}_{H}", [128, 2 * 2048], F32, p2)
            psD = [[pt(f"psD{i}{tau}_{b}_{H}", [128, 1024], F32, p2) for tau in range(2)]
                   for i in range(2)]
            down_base = S.cnt[s_down]
            ap3_base = S.cnt[s_ap3]
            xadd3_base = S.cnt[s_xadd3]
            hT_total = S.cnt[s_hT]
            wdT_ready = (q_wdT, 16 * (b + 1))
            add_d = {}
            for Sc in range(4):
                slot = Sc % 2
                for ht in range(NHT):
                    for tau in range(2):
                        for d2h in range(2):
                            waits = []
                            if ht == 0 and tau == 0 and d2h == 0:
                                waits = [(s_hT, hT_total), wdT_ready,
                                         (s_ap3, ap3_base + 2 * (Sc - 2) + 2)]
                            S.add("tensor", lambda tt, slot=slot, tau=tau, d2h=d2h, ht=ht, Sc=Sc: tt.matmul(
                                psD[slot][tau][:, 512 * d2h:512 * (d2h + 1)],
                                hT[:, 1024 * ht + 256 * Sc + 128 * tau:
                                   1024 * ht + 256 * Sc + 128 * (tau + 1)],
                                wdT[:, 1024 * ht + 512 * d2h:1024 * ht + 512 * (d2h + 1)],
                                start=(ht == 0), stop=(ht == NHT - 1)),
                                waits=waits,
                                inc=(s_down if (ht == NHT - 1 and tau == 1 and d2h == 1) else None))
                dn = S.cnt[s_down]
                for tau in range(2):
                    S.add("vector", lambda v, slot=slot, tau=tau: v.bn_stats(
                        bn2[:, 12 * tau:12 * tau + 6], psD[slot][tau][:, 0:512]),
                        waits=[(s_down, dn),
                               (s_ap3, ap3_base + 2 * (Sc - 1) + 2)] if tau == 0 else [])
                    last3 = S.add("vector", lambda v, slot=slot, tau=tau: v.bn_stats(
                        bn2[:, 12 * tau + 6:12 * tau + 12], psD[slot][tau][:, 512:1024]),
                        inc=c_dve)
                for tau in range(2):
                    last3 = S.add("vector", lambda v, tau=tau: v.bn_aggr(
                        mv2[:, 2 * tau:2 * tau + 2], bn2[:, 12 * tau:12 * tau + 12]),
                        waits=[(c_dve, last3)] if tau == 0 else [], inc=c_dve)
                var3 = bass.AP(mv2, 1, [[4, 128], [2, 2]])
                mean3 = bass.AP(mv2, 0, [[4, 128], [2, 2]])
                v3 = S.add("vector", lambda v: v.tensor_scalar(
                    vfix2[:, :], var3, EPS3, None, AL.add),
                    waits=[(c_dve, last3)], inc=c_dve)
                l3 = S.add("scalar", lambda sc: sc.activation(rstd2[:, :], vfix2[:, :], AF.Ln),
                           waits=[(c_dve, v3)], inc=c_act)
                e3 = S.add("scalar", lambda sc: sc.activation(rstd2[:, :], rstd2[:, :], AF.Exp, scale=-0.5),
                           waits=[(c_act, l3)], inc=c_act)
                n31 = S.add("vector", lambda v: v.tensor_tensor(nmr2[:, :], mean3, rstd2[:, :], AL.mult),
                            waits=[(c_act, e3)], inc=c_dve)
                n32 = S.add("vector", lambda v: v.tensor_scalar(nmr2[:, :], nmr2[:, :], -1.0, None, AL.mult),
                            waits=[(c_dve, n31)], inc=c_dve)
                if H == 0 and Sc in (1, 2):
                    hoist_b1_stats(b, 1, Sc - 1)
                if b == 0 and H == 1 and Sc in (1, 2):
                    hoist_a1_stats(1, Sc - 1)
                if b == 0 and H == 1 and Sc == 2:
                    hoist_a1_stats(1, 2)
                for tau in range(2):
                    S.add("scalar", lambda sc, slot=slot, tau=tau, Sc=Sc: sc.activation(
                        u2[:, 2048 * slot + 1024 * tau:2048 * slot + 1024 * (tau + 1)],
                        psD[slot][tau][:, :], AF.Identity,
                        bias=nmr2[:, tau:tau + 1], scale=rstd2[:, tau:tau + 1]),
                        waits=[(c_dve, n32),
                               (s_xadd3, xadd3_base + 2 * (Sc - 2) + tau + 1)],
                        inc=s_ap3)
                for tau in range(2):
                    tt_idx = 8 * H + 2 * Sc + tau
                    if Sc == 3:
                        add_d[tau] = S.add("vector", lambda v, slot=slot, tau=tau, tt_idx=tt_idx: v.tensor_tensor(
                            x_sb[:, D * tt_idx:D * (tt_idx + 1)],
                            x_sb[:, D * tt_idx:D * (tt_idx + 1)],
                            u2[:, 2048 * slot + 1024 * tau:2048 * slot + 1024 * (tau + 1)],
                            AL.add),
                            waits=[(s_ap3, ap3_base + 2 * Sc + tau + 1)], inc=s_xadd3d)
                    else:
                        S.add("gpsimd", lambda p, slot=slot, tau=tau, tt_idx=tt_idx: p.tensor_tensor(
                            x_sb[:, D * tt_idx:D * (tt_idx + 1)],
                            x_sb[:, D * tt_idx:D * (tt_idx + 1)],
                            u2[:, 2048 * slot + 1024 * tau:2048 * slot + 1024 * (tau + 1)],
                            AL.add),
                            waits=[(s_ap3, ap3_base + 2 * Sc + tau + 1)], inc=s_xadd3)
                for tau in range(2):
                    tt_idx = 8 * H + 2 * Sc + tau
                    stw = ([(s_xadd3, xadd3_base + 6), (s_xadd3d, add_d[tau])]
                           if Sc == 3 else
                           [(s_xadd3, xadd3_base + 2 * Sc + tau + 1)])
                    S.add("sync", lambda s, tt_idx=tt_idx, b=b: s.dma_start(
                        out_hbm[b, 128 * tt_idx:128 * (tt_idx + 1), :],
                        x_sb[:, D * tt_idx:D * (tt_idx + 1)]),
                        waits=stw,
                        inc=q_st[tt_idx // 4], amt=16)
                if b == 0 and Sc % 2 == 1:
                    grp = 2 * H + (Sc - 1) // 2
                    xld_counts[1][grp] = emit_xload(1, grp)
            S.flush(f"b2_{b}_{H}")
            p2.close()
            wds.close()
            fh.close()
        fb.close()

    S.add("sync", None, waits=[(q_st[i], S.cnt[q_st[i]]) for i in range(4)])
    S.flush("fin")
    es.close()
    return nc


_CACHE = {}


def kernel(L0_attn_x, L0_attn_Mq, L0_attn_Mk, L0_attn_Mv,
           L0_ffn_w_gate, L0_ffn_w_up, L0_ffn_w_down):
    x = np.ascontiguousarray(np.asarray(L0_attn_x, dtype=np.float32))
    mq = np.ascontiguousarray(np.asarray(L0_attn_Mq, dtype=np.float32))
    mk = np.ascontiguousarray(np.asarray(L0_attn_Mk, dtype=np.float32))
    mv = np.ascontiguousarray(np.asarray(L0_attn_Mv, dtype=np.float32))
    wg = np.ascontiguousarray(np.asarray(L0_ffn_w_gate, dtype=np.int32))
    wu = np.ascontiguousarray(np.asarray(L0_ffn_w_up, dtype=np.int32))
    wd = np.ascontiguousarray(np.asarray(L0_ffn_w_down, dtype=np.int32))

    if "nc" not in _CACHE:
        _CACHE["nc"] = build()
    nc = _CACHE["nc"]

    in_maps = []
    for c in range(8):
        in_maps.append({
            "x": x[NB * c:NB * (c + 1)],
            "mq": mq, "mk": mk, "mv": mv,
            "wg": wg, "wu": wu, "wd": wd,
        })
    res = run_bass_kernel_spmd(nc, in_maps, core_ids=list(range(8)))
    return np.concatenate([r["out"] for r in res.results], axis=0)




# version 35
# speedup vs baseline: 1.0111x; 1.0111x over previous
import numpy as np
from contextlib import ExitStack

import concourse.bass as bass
import concourse.mybir as mybir
from concourse.bass_utils import run_bass_kernel_spmd
from concourse.alu_op_type import AluOpType

F32 = mybir.dt.float32
BF16 = mybir.dt.bfloat16
I32 = mybir.dt.int32
AF = mybir.ActivationFunctionType
AL = AluOpType

NB = 2
NT = 16
D = 1024
HID = 3072
NHT = 24
EPS = 1e-5
EPS3 = 1e-5 * 16129.0 ** 2


class Sched:

    ENGINES = ("tensor", "scalar", "vector", "gpsimd", "sync")

    def __init__(self, nc, es):
        self.nc = nc
        self.es = es
        self.cnt = {}
        self.h = {}

    def sem(self, name):
        self.h[name] = self.es.enter_context(self.nc.semaphore(name))
        self.cnt[name] = 0
        return name

    def begin(self):
        self.ops = {e: [] for e in self.ENGINES}

    def add(self, eng, fn, waits=(), inc=None, amt=1):
        w = [(s, v) for (s, v) in waits if v > 0]
        self.ops[eng].append((w, fn, inc, amt))
        if inc is not None:
            self.cnt[inc] += amt
            return self.cnt[inc]
        return None

    def flush(self, name=None):
        nc, h = self.nc, self.h
        self.nflush = getattr(self, "nflush", 0) + 1
        if getattr(self, "stop_after", None) is not None and self.nflush > self.stop_after:
            self.begin()
            return
        ops = self.ops
        with nc.Block(name) as block:
            for eng_name in self.ENGINES:
                lst = ops[eng_name]
                if not lst:
                    continue

                def body(eng, lst=lst):
                    for w, fn, inc, amt in lst:
                        for s, v in w:
                            eng.wait_ge(h[s], v)
                        if fn is None:
                            continue
                        instr = fn(eng)
                        if inc is not None:
                            instr.then_inc(h[inc], amt)

                getattr(block, eng_name)(body)
        self.begin()


def build(stop_after=None, debug_dump=False):
    nc = bass.Bass("TRN2", target_bir_lowering=False, dynamic_dma_scratch_size=16384)
    x_hbm = nc.dram_tensor("x", [NB, 2048, D], F32, kind="ExternalInput").ap()
    mq_hbm = nc.dram_tensor("mq", [D, 512], F32, kind="ExternalInput").ap()
    mk_hbm = nc.dram_tensor("mk", [D, 512], F32, kind="ExternalInput").ap()
    mv_hbm = nc.dram_tensor("mv", [D, D], F32, kind="ExternalInput").ap()
    wg_hbm = nc.dram_tensor("wg", [HID, D], I32, kind="ExternalInput").ap()
    wu_hbm = nc.dram_tensor("wu", [HID, D], I32, kind="ExternalInput").ap()
    wd_hbm = nc.dram_tensor("wd", [D, HID], I32, kind="ExternalInput").ap()
    out_hbm = nc.dram_tensor("out", [NB, 2048, D], F32, kind="ExternalOutput").ap()
    wgT_dram = nc.dram_tensor("wgT_scratch", [NHT, 128, 1024], BF16, kind="Internal").ap()
    wuT_dram = nc.dram_tensor("wuT_scratch", [NHT, 128, 1024], BF16, kind="Internal").ap()
    wdT_dram = nc.dram_tensor("wdT_scratch", [NHT, 128, 1024], BF16, kind="Internal").ap()

    es = ExitStack()
    S = Sched(nc, es)
    S.stop_after = stop_after

    def sb(name, shape, dt, scope=es):
        return scope.enter_context(nc.sbuf_tensor(name, shape, dt))

    def pt(name, shape, dt, scope=es):
        return scope.enter_context(nc.psum_tensor(name, shape, dt))

    q_x = [S.sem(f"q_x{i}") for i in range(4)]
    q_x0 = S.sem("q_xe")
    q_st = [S.sem(f"q_st{i}") for i in range(4)]
    q_wq = S.sem("q_wq")
    q_wk = S.sem("q_wk")
    q_wks = S.sem("q_wks")
    q_wv = S.sem("q_wv")
    q_wv1 = S.sem("q_wv1")
    q_ksw = S.sem("q_ksw")
    q_wg = [S.sem(f"q_wg{i}") for i in range(2)]
    q_wu = [S.sem(f"q_wu{i}") for i in range(2)]
    q_wst = [S.sem(f"q_wst{i}") for i in range(4)]
    q_wstd = [S.sem(f"q_wstd{i}") for i in range(2)]
    q_wgw = [S.sem(f"q_wgw{i}") for i in range(2)]
    q_wuw = [S.sem(f"q_wuw{i}") for i in range(2)]
    q_wtr = [S.sem(f"q_wtr{i}") for i in range(2)]
    q_wgt = [S.sem(f"q_wgt{i}") for i in range(2)]
    q_wut = [S.sem(f"q_wut{i}") for i in range(2)]
    q_wgi = [S.sem(f"q_wgi{i}") for i in range(2)]
    q_wui = [S.sem(f"q_wui{i}") for i in range(2)]
    q_wdi = [S.sem(f"q_wdi{i}") for i in range(2)]
    q_wdw = [S.sem(f"q_wdw{i}") for i in range(4)]
    q_wdT = S.sem("q_wdT")
    c_act = S.sem("c_act")
    c_dve = S.sem("c_dve")
    c_pool = S.sem("c_pool")
    s_xn = S.sem("s_xn")
    s_tp = S.sem("s_tp")
    s_evT = S.sem("s_evT")
    s_evD = S.sem("s_evD")
    s_proj = S.sem("s_proj")
    s_evP = S.sem("s_evP")
    s_evPA = S.sem("s_evPA")
    s_mm = S.sem("s_mm")
    s_exp = S.sem("s_exp")
    s_scale = S.sem("s_scale")
    s_av = S.sem("s_av")
    s_apply = S.sem("s_apply")
    s_xadd = S.sem("s_xadd")
    s_wT = S.sem("s_wT")
    s_wgev = S.sem("s_wgev")
    s_wuev = S.sem("s_wuev")
    s_wdev = S.sem("s_wdev")
    s_gup = S.sem("s_gup")
    s_silu = S.sem("s_silu")
    s_hT = S.sem("s_hT")
    s_down = S.sem("s_down")
    s_ap3 = S.sem("s_ap3")
    s_xadd3 = S.sem("s_xadd3")
    s_xadd3d = S.sem("s_xadd3d")
    s_xaddd = S.sem("s_xaddd")

    S.begin()

    ident = sb("ident", [128, 128], BF16)
    setup_es = ExitStack()
    io_a = sb("io_a", [128, 128], F32, setup_es)
    io_b = sb("io_b", [128, 128], F32, setup_es)
    S.add("gpsimd", lambda g: g.iota(io_a[:, :], [[1, 128]], channel_multiplier=0,
                                     allow_small_or_imprecise_dtypes=True), inc=c_pool)
    S.add("gpsimd", lambda g: g.iota(io_b[:, :], [[0, 128]], channel_multiplier=1,
                                     allow_small_or_imprecise_dtypes=True), inc=c_pool)
    S.add("vector", lambda v: v.tensor_tensor(ident[:, :], io_a[:, :], io_b[:, :], AL.is_equal),
          waits=[(c_pool, 2)], inc=c_dve)
    S.flush("setup")
    setup_es.close()

    x_sb = sb("x_sb", [128, NT * D], F32)
    xn_stage = sb("xn_stage", [128, 2 * D], BF16)
    bn1 = sb("bn1", [128, 12 * NT], F32)
    mvs = sb("mvs", [128, 2 * NT], F32)
    vfix1 = sb("vfix1", [128, NT], F32)
    rstd1 = sb("rstd1", [128, NT], F32)
    nmr1 = sb("nmr1", [128, NT], F32)
    accx = sb("accx", [128, 16], F32)
    ssum = sb("ssum", [128, 8], F32)
    rsx = sb("rsx", [128, 8], F32)
    bn2 = sb("bn2", [128, 24], F32)
    mv2 = sb("mv2", [128, 4], F32)
    vfix2 = sb("vfix2", [128, 2], F32)
    rstd2 = sb("rstd2", [128, 2], F32)
    nmr2 = sb("nmr2", [128, 2], F32)

    xld_counts = {0: {}, 1: {}}
    b1_stats_last = {}
    a2_add_bases = {}

    a1_stats_last = {}

    def hoist_a1_stats(bb, g4):
        last = None
        for t in range(4 * g4, 4 * g4 + 4):
            S.add("vector", lambda v, t=t: v.bn_stats(
                bn1[:, 12 * t:12 * t + 6], x_sb[:, D * t:D * t + 512]),
                waits=[(q_x[t // 4], xld_counts[bb][t // 4])])
            last = S.add("vector", lambda v, t=t: v.bn_stats(
                bn1[:, 12 * t + 6:12 * t + 12], x_sb[:, D * t + 512:D * t + 1024]),
                inc=c_dve)
        for t in range(4 * g4, 4 * g4 + 4):
            last = S.add("vector", lambda v, t=t: v.bn_aggr(
                mvs[:, 2 * t:2 * t + 2], bn1[:, 12 * t:12 * t + 12]),
                waits=[(c_dve, last)] if t == 4 * g4 else [], inc=c_dve)
        a1_stats_last[(bb, g4)] = last

    def hoist_b1_stats(bb, HH, hf):
        xab, xdb = a2_add_bases[bb]
        last = None
        for j in range(4 * hf, 4 * hf + 4):
            t = 8 * HH + j
            xw = (s_xadd, xab + t + 1) if t <= 13 else (s_xaddd, xdb + t - 13)
            S.add("vector", lambda v, j=j, t=t: v.bn_stats(
                bn1[:, 12 * j:12 * j + 6], x_sb[:, D * t:D * t + 512]),
                waits=[xw])
            last = S.add("vector", lambda v, j=j, t=t: v.bn_stats(
                bn1[:, 12 * j + 6:12 * j + 12], x_sb[:, D * t + 512:D * t + 1024]),
                inc=c_dve)
        for j in range(4 * hf, 4 * hf + 4):
            last = S.add("vector", lambda v, j=j: v.bn_aggr(
                mvs[:, 2 * j:2 * j + 2], bn1[:, 12 * j:12 * j + 12]),
                waits=[(c_dve, last)] if j == 4 * hf else [], inc=c_dve)
        b1_stats_last[(bb, HH, hf)] = last

    def emit_xload(bb, i):
        dst = bass.AP(x_sb, 4096 * i, [[NT * D, 128], [D, 4], [1, D]])
        srcx = bass.AP(x_hbm.tensor, bb * 2048 * D + 512 * i * D,
                       [[D, 128], [128 * D, 4], [1, D]])
        w = [(q_st[i], S.cnt[q_st[i]])] if bb > 0 else []
        return S.add("sync", lambda s, dst=dst, srcx=srcx: s.dma_start(dst, srcx),
                     waits=w, inc=q_x[i], amt=16)

    for b in range(NB):
        bs = ExitStack()

        qT = sb(f"qT_{b}", [128, 4 * 2048], BF16, bs)
        kT = sb(f"kT_{b}", [128, 4 * 2048], BF16, bs)
        kTs = sb(f"kTs_{b}", [128, 4 * 2048], BF16, bs)
        v_sb = sb(f"v_{b}", [128, NT * D], BF16, bs)

        a1 = ExitStack()
        xn_bf = xn_stage
        xnT = sb(f"xnT_{b}", [128, 8 * 2048], BF16, a1)
        w_pg = [sb(f"wpg{i}_{b}", [128, 8 * 512], BF16, a1) for i in range(2)]
        wks_sb = sb(f"wks_{b}", [128, 8 * 512], BF16, a1)
        ps_tp = [pt(f"pstp{i}_{b}", [128, 1024], BF16, a1) for i in range(2)]
        psP = [pt(f"psP{i}_{b}", [128, 512], F32, a1) for i in range(4)]

        if b == 1:
            for e in ("sync", "gpsimd", "vector", "scalar"):
                S.add(e, None, waits=[(q_wdT, S.cnt[q_wdT])])
        if b == 0:
            S.add("sync", lambda s: s.dma_start(
                bass.AP(x_sb, 0, [[NT * D, 128], [D, 2], [1, D]]),
                bass.AP(x_hbm.tensor, 0, [[D, 128], [128 * D, 2], [1, D]])),
                inc=q_x0, amt=16)
            xld_counts[0][0] = S.add("sync", lambda s: s.dma_start(
                bass.AP(x_sb, 2048, [[NT * D, 128], [D, 2], [1, D]]),
                bass.AP(x_hbm.tensor, 2 * 128 * D, [[D, 128], [128 * D, 2], [1, D]])),
                inc=q_x[0], amt=16)
            for i in range(1, 4):
                xld_counts[0][i] = emit_xload(0, i)
        xld = xld_counts[b]
        dst_q = bass.AP(w_pg[0], 0, [[4096, 128], [512, 8], [1, 512]])
        src_q = bass.AP(mq_hbm.tensor, 0, [[512, 128], [65536, 8], [1, 512]])
        wq_ld = S.add("gpsimd", lambda g: g.dma_start(dst_q, src_q), inc=q_wq, amt=16)
        dst_k = bass.AP(w_pg[1], 0, [[4096, 128], [512, 8], [1, 512]])
        src_k = bass.AP(mk_hbm.tensor, 0, [[512, 128], [65536, 8], [1, 512]])
        wk_ld = S.add("gpsimd", lambda g: g.dma_start(dst_k, src_k), inc=q_wk, amt=16)
        dstv1 = bass.AP(kTs, 0, [[8192, 128], [1024, 4], [1, 1024]])
        srcv1 = bass.AP(mv_hbm.tensor, 4 * 128 * D,
                        [[D, 128], [128 * D, 4], [1, D]])
        wv1_early = S.add("gpsimd", lambda g, dstv1=dstv1, srcv1=srcv1: g.dma_start(
            dstv1, srcv1), inc=q_wv1, amt=16)

        tp_base = S.cnt[s_tp]
        evT_base = S.cnt[s_evT]
        xn_base = S.cnt[s_xn]
        for g4 in range(4):
            tiles = [4 * g4 + j for j in range(4)]
            if (b, g4) in a1_stats_last:
                aggr_done = a1_stats_last[(b, g4)]
            else:
                for t in tiles:
                    xw = (q_x0, 16) if (b == 0 and t < 2) else (q_x[t // 4], xld[t // 4])
                    S.add("vector", lambda v, t=t: v.bn_stats(bn1[:, 12 * t:12 * t + 6],
                                                              x_sb[:, D * t:D * t + 512]),
                          waits=[xw])
                    stats_done = S.add("vector", lambda v, t=t: v.bn_stats(
                        bn1[:, 12 * t + 6:12 * t + 12], x_sb[:, D * t + 512:D * t + 1024]),
                        inc=c_dve)
                for t in tiles:
                    aggr_done = S.add("vector", lambda v, t=t: v.bn_aggr(
                        mvs[:, 2 * t:2 * t + 2], bn1[:, 12 * t:12 * t + 12]),
                        waits=[(c_dve, stats_done)] if t == tiles[0] else [], inc=c_dve)
            var_v4 = bass.AP(mvs, 8 * g4 + 1, [[2 * NT, 128], [2, 4]])
            mean_v4 = bass.AP(mvs, 8 * g4, [[2 * NT, 128], [2, 4]])
            veps_done = S.add("vector", lambda v, g4=g4, var_v4=var_v4: v.tensor_scalar(
                vfix1[:, 4 * g4:4 * g4 + 4], var_v4, EPS, None, AL.add),
                waits=[(c_dve, aggr_done)], inc=c_dve)
            ln_done = S.add("scalar", lambda sc, g4=g4: sc.activation(
                rstd1[:, 4 * g4:4 * g4 + 4], vfix1[:, 4 * g4:4 * g4 + 4], AF.Ln),
                waits=[(c_dve, veps_done)], inc=c_act)
            rstd_done = S.add("scalar", lambda sc, g4=g4: sc.activation(
                rstd1[:, 4 * g4:4 * g4 + 4], rstd1[:, 4 * g4:4 * g4 + 4], AF.Exp, scale=-0.5),
                waits=[(c_act, ln_done)], inc=c_act)
            negm_done = S.add("vector", lambda v, g4=g4, mean_v4=mean_v4: v.tensor_scalar(
                nmr1[:, 4 * g4:4 * g4 + 4], mean_v4, -1.0, None, AL.mult),
                waits=[(c_act, rstd_done)], inc=c_dve)
            for t in tiles:
                S.add("vector", lambda v, t=t: v.tensor_scalar(
                    xn_bf[:, D * (t % 2):D * (t % 2 + 1)], x_sb[:, D * t:D * (t + 1)],
                    nmr1[:, t:t + 1], rstd1[:, t:t + 1], AL.add, AL.mult),
                    waits=[(c_dve, negm_done), (s_tp, tp_base + t - 1)], inc=s_xn)
                for dt in range(8):
                    S.add("tensor", lambda tt, t=t, dt=dt: tt.transpose(
                        ps_tp[t % 2][:, 128 * dt:128 * (dt + 1)],
                        xn_bf[:, D * (t % 2) + 128 * dt:D * (t % 2) + 128 * (dt + 1)],
                        ident[:, :]),
                        waits=([(s_xn, xn_base + t + 1), (s_evT, evT_base + t - 1)]
                               if dt == 0 else []),
                        inc=(s_tp if dt == 7 else None))
                dest = bass.AP(xnT, 128 * t, [[8 * 2048, 128], [2048, 8], [1, 128]])
                S.add("scalar", lambda sc, t=t, dest=dest: sc.copy(dest, ps_tp[t % 2][:, :]),
                      waits=[(s_tp, tp_base + t + 1)], inc=s_evT)

        proj_groups = []
        for q in range(4):
            for c in range(4):
                proj_groups.append(("q", q, c))
        for q in range(4):
            for c in range(4):
                proj_groups.append(("k", q, c))
        for t in range(NT):
            for c2 in range(2):
                proj_groups.append(("v", t, c2))

        wv_ld = None
        proj_base = S.cnt[s_proj]
        evP_base = S.cnt[s_evP]
        evPA_base = S.cnt[s_evPA]

        if b == 0:
            wd_ld = {}
            wdev0 = S.cnt[s_wdev]
            wdtp0 = S.cnt[s_wT]
            wd_emitted = [0]

            def wd_stage_dma(u):
                d2t, grp = u // 3, u % 3
                w = [(s_tp, tp_base + NT)] if u < 2 else [(s_wT, wdtp0 + u - 1)]
                wd_ld[u] = S.add("gpsimd", lambda g, d2t=d2t, grp=grp, u=u: g.dma_start(
                    xn_stage[:, 1024 * (u % 2):1024 * (u % 2 + 1)],
                    wd_hbm[128 * d2t:128 * (d2t + 1), 1024 * grp:1024 * (grp + 1)]),
                    waits=w, inc=q_wdi[u % 2], amt=16)

            def wd_unit(u):
                d2t, grp = u // 3, u % 3
                sl = u % 2
                for j in range(8):
                    w = []
                    if j == 0:
                        w = [(q_wdi[sl], wd_ld[u]), (s_evT, evT_base + NT)]
                        if u >= 2:
                            w.append((s_wdev, wdev0 + u - 1))
                    S.add("tensor", lambda tt, sl=sl, j=j: tt.transpose(
                        ps_tp[sl][:, 128 * j:128 * (j + 1)],
                        xn_stage[:, 1024 * sl + 128 * j:1024 * sl + 128 * (j + 1)],
                        ident[:, :]),
                        waits=w, inc=(s_wT if j == 7 else None))
                sl4 = u % 4
                evw = [(s_wT, wdtp0 + u + 1)]
                if u >= 4:
                    evw.append((q_wdw[sl4], 16 * ((u - 4) // 4 + 1)))
                ev = S.add("vector", lambda v, sl4=sl4: v.tensor_copy(
                    wks_sb[:, 1024 * sl4:1024 * (sl4 + 1)], ps_tp[sl][:, :]),
                    waits=evw, inc=s_wdev)
                dstw = bass.AP(wdT_dram.tensor, (8 * grp) * 128 * 1024 + 128 * d2t,
                               [[1024, 128], [128 * 1024, 8], [1, 128]])
                S.add("sync", lambda s, dstw=dstw, sl4=sl4: s.dma_start(
                    dstw, wks_sb[:, 1024 * sl4:1024 * (sl4 + 1)]),
                    waits=[(s_wdev, ev)], inc=q_wdw[sl4], amt=16)

            def wd_tick(frac):
                want = min(24, int(frac * 24))
                if want > 0 and wd_emitted[0] == 0:
                    wd_stage_dma(0)
                    wd_stage_dma(1)
                while wd_emitted[0] < want:
                    u = wd_emitted[0]
                    if u + 2 < 24:
                        wd_stage_dma(u + 2)
                    wd_unit(u)
                    wd_emitted[0] += 1
        else:
            def wd_tick(frac):
                pass

        for gi, (kind, i1, i2) in enumerate(proj_groups):
            if kind == "k" and i1 == 0 and i2 == 0:
                dstv = bass.AP(w_pg[0], 0, [[4096, 128], [1024, 4], [1, 1024]])
                srcv = bass.AP(mv_hbm.tensor, 0, [[D, 128], [128 * D, 4], [1, D]])
                wv_lds = [S.add(
                    "gpsimd", lambda g, dstv=dstv, srcv=srcv: g.dma_start(dstv, srcv),
                    waits=[(s_proj, proj_base + 16)], inc=q_wv, amt=16),
                    wv1_early]

            slot = gi % 4
            for dt in range(8):
                if kind == "q":
                    lhsT = w_pg[0][:, 512 * dt + 128 * i1:512 * dt + 128 * (i1 + 1)]
                    rhs = xnT[:, 2048 * dt + 512 * i2:2048 * dt + 512 * (i2 + 1)]
                    wwait = (q_wq, wq_ld)
                elif kind == "k":
                    lhsT = w_pg[1][:, 512 * dt + 128 * i1:512 * dt + 128 * (i1 + 1)]
                    rhs = xnT[:, 2048 * dt + 512 * i2:2048 * dt + 512 * (i2 + 1)]
                    wwait = (q_wk, wk_ld)
                else:
                    lhsT = xnT[:, 2048 * dt + 128 * i1:2048 * dt + 128 * (i1 + 1)]
                    wslot = w_pg[0] if dt < 4 else kTs
                    col = 1024 * (dt % 4) + 512 * i2
                    rhs = wslot[:, col:col + 512]
                    wwait = (q_wv, wv_lds[0])
                waits = []
                if kind == "v" and dt == 4:
                    waits = [(q_wv1, wv_lds[1])]
                if dt == 0:
                    if kind == "v":
                        evt_need = evT_base + i1 + 1
                    else:
                        evt_need = evT_base + 4 * i2 + 4
                    waits = [wwait, (s_evT, evt_need)]
                    if gi >= 4:
                        if gi % 2 == 0:
                            waits.append((s_evP, evP_base + gi // 2 - 1))
                        else:
                            waits.append((s_evPA, evPA_base + (gi - 3) // 2))
                S.add("tensor", lambda tt, lhsT=lhsT, rhs=rhs, slot=slot, dt=dt: tt.matmul(
                    psP[slot][:, :], lhsT, rhs, start=(dt == 0), stop=(dt == 7)),
                    waits=waits, inc=(s_proj if dt == 7 else None))
            if kind == "q":
                dst = qT[:, 2048 * i1 + 512 * i2: 2048 * i1 + 512 * (i2 + 1)]
            elif kind == "k":
                dst = kT[:, 2048 * i1 + 512 * i2: 2048 * i1 + 512 * (i2 + 1)]
            else:
                dst = v_sb[:, D * i1 + 512 * i2: D * i1 + 512 * (i2 + 1)]
            if gi % 2 == 0:
                S.add("vector", lambda v, dst=dst, slot=slot: v.tensor_copy(dst, psP[slot][:, :]),
                      waits=[(s_proj, proj_base + gi + 1)], inc=s_evP)
            else:
                S.add("scalar", lambda sc, dst=dst, slot=slot: sc.copy(dst, psP[slot][:, :]),
                      waits=[(s_proj, proj_base + gi + 1)], inc=s_evPA)
            if kind == "k" and i2 == 3 and i1 >= 2:
                kw = [(s_evP, evP_base + 10 + 2 * i1), (s_evPA, evPA_base + 10 + 2 * i1)]
                S.add("sync", lambda s, i1=i1: s.dma_start(
                    kTs[0:64, 2048 * i1:2048 * (i1 + 1)],
                    kT[64:128, 2048 * i1:2048 * (i1 + 1)]),
                    waits=kw, inc=q_ksw, amt=16)
                S.add("sync", lambda s, i1=i1: s.dma_start(
                    kTs[64:128, 2048 * i1:2048 * (i1 + 1)],
                    kT[0:64, 2048 * i1:2048 * (i1 + 1)]),
                    inc=q_ksw, amt=16)
            if gi >= 32:
                wd_tick((gi - 31) / 32.0)
        wd_tick(1.0)
        for i1 in range(2):
            S.add("sync", lambda s, i1=i1: s.dma_start(
                kTs[0:64, 2048 * i1:2048 * (i1 + 1)],
                kT[64:128, 2048 * i1:2048 * (i1 + 1)]),
                waits=[(s_proj, proj_base + 64)] if i1 == 0 else [], inc=q_ksw, amt=16)
            S.add("sync", lambda s, i1=i1: s.dma_start(
                kTs[64:128, 2048 * i1:2048 * (i1 + 1)],
                kT[0:64, 2048 * i1:2048 * (i1 + 1)]),
                inc=q_ksw, amt=16)
        if debug_dump and b == 0:
            S.add("gpsimd", lambda g: g.dma_start(out_hbm[1, 0:128, :], x_sb[:, 0:D]),
                  waits=[(s_evP, S.cnt[s_evP]), (s_evPA, S.cnt[s_evPA])], inc=q_st[0], amt=16)
            S.add("gpsimd", lambda g: g.dma_start(out_hbm[1, 128:256, :], xnT[:, 0:D]),
                  inc=q_st[0], amt=16)
            S.add("gpsimd", lambda g: g.dma_start(out_hbm[1, 256:384, :], qT[:, 0:D]),
                  inc=q_st[0], amt=16)
            S.add("gpsimd", lambda g: g.dma_start(out_hbm[1, 384:512, :], kT[:, 0:D]),
                  inc=q_st[0], amt=16)
            S.add("gpsimd", lambda g: g.dma_start(out_hbm[1, 512:640, :], kTs[:, 0:D]),
                  inc=q_st[0], amt=16)
            S.add("gpsimd", lambda g: g.dma_start(out_hbm[1, 640:768, :], v_sb[:, 0:D]),
                  inc=q_st[0], amt=16)
            S.add("sync", None, waits=[(q_st[0], S.cnt[q_st[0]])])
        S.flush(f"a1_{b}")
        a1.close()

        if b == 0:
            for e in ("scalar", "vector", "gpsimd"):
                S.add(e, None, waits=[(q_wdw[i], S.cnt[q_wdw[i]]) for i in range(4)])
        a2 = ExitStack()
        att = sb(f"att_{b}", [128, 8 * 2048], BF16, a2)
        u_st = sb(f"u_{b}", [128, 2 * 2048], BF16, a2)
        Sp = [pt(f"S{i}_{b}", [128, 1024], F32, a2) for i in range(2)]
        Op = [pt(f"O{i}_{b}", [128, 1024], F32, a2) for i in range(2)]

        mm_base = S.cnt[s_mm]
        exp_base = S.cnt[s_exp]
        scale_base = S.cnt[s_scale]
        av_base = S.cnt[s_av]
        apply_base = S.cnt[s_apply]
        xadd_base = S.cnt[s_xadd]
        a2_add_bases[b] = (xadd_base, S.cnt[s_xaddd])

        def emit_av_step(g, t):
            h = t // 2
            sl6 = (16 * g + t) % 8
            for m in range(8):
                for dlt in range(2):
                    lhsT = att[:, 2048 * sl6 + 256 * m + 128 * dlt:
                               2048 * sl6 + 256 * m + 128 * (dlt + 1)]
                    rhs = v_sb[:, D * (2 * m + t % 2) + 128 * h:
                               D * (2 * m + t % 2) + 128 * (h + 1)]
                    waits = []
                    if m == 0 and dlt == 0:
                        waits = [(s_scale, scale_base + 16 * g + t + 1)]
                        if t == 0:
                            waits.append((s_apply, apply_base + 2 * g))
                    inc = s_av if (m == 7 and dlt == 1) else None
                    S.add("tensor", lambda tt, lhsT=lhsT, rhs=rhs, dlt=dlt, m=m, t=t: tt.matmul(
                        Op[dlt][:, 128 * m:128 * (m + 1)], lhsT, rhs,
                        start=(t == 0 and m % 4 == 0), stop=(t == 15 and m % 4 == 3)),
                        waits=waits, inc=inc)

        pending_rstd = None
        pending_dve_tail = None

        def emit_rstd(pr):
            l2 = S.add("scalar", lambda sc: sc.activation(rstd2[:, :], vfix2[:, :], AF.Ln),
                       waits=[(c_dve, pr[0])], inc=c_act)
            e2 = S.add("scalar", lambda sc: sc.activation(rstd2[:, :], rstd2[:, :],
                                                          AF.Exp, scale=-0.5),
                       waits=[(c_act, l2)], inc=c_act)
            assert e2 == pr[1], (e2, pr)

        for g in range(8):
            for k in range(32):
                if g > 0 and k in (0, 2, 4, 6, 8):
                    emit_av_step(g - 1, 11 + k // 2)
                K = 32 * g + k
                t, c = k // 2, k % 2
                hh = t // 2
                for mi in range(4):
                    m = 2 * mi + c
                    if (hh % 2) == (m % 2):
                        ksrc, kbase = kT, 64 * (hh % 2)
                    else:
                        ksrc, kbase = kTs, 64 * (1 - hh % 2)
                    lhsT = ksrc[kbase:kbase + 64,
                                2048 * (hh // 2) + 256 * g + 128 * (t % 2):
                                2048 * (hh // 2) + 256 * g + 128 * (t % 2) + 128]
                    rhs = qT[64 * (m % 2):64 * (m % 2) + 64,
                             2048 * (m // 2) + 256 * g:2048 * (m // 2) + 256 * g + 256]
                    waits = []
                    if mi == 0:
                        waits = [(s_exp, exp_base + K - 1)]
                        if g == 0 and k == 1:
                            waits.append((q_ksw, S.cnt[q_ksw]))
                    S.add("tensor", lambda tt, lhsT=lhsT, rhs=rhs, mi=mi, K=K: tt.matmul(
                        Sp[K % 2][:, 256 * mi:256 * (mi + 1)], lhsT, rhs,
                        start=True, stop=True),
                        waits=waits, inc=(s_mm if mi == 3 else None))
                if k >= 15 and k % 2 == 1:
                    emit_av_step(g, (k - 15) // 2)
            emit_av_step(g, 9)
            emit_av_step(g, 10)
            if g == 7:
                for tav in (11, 12, 13, 14, 15):
                    emit_av_step(g, tav)

            def emit_tile_chain(t, g=g):
                sl6 = (16 * g + t) % 8
                w1 = S.add("vector", lambda v, sl6=sl6: v.tensor_tensor(
                    ssum[:, sl6:sl6 + 1], accx[:, 2 * sl6:2 * sl6 + 1],
                    accx[:, 2 * sl6 + 1:2 * sl6 + 2], AL.add),
                    waits=[(s_exp, exp_base + 32 * g + 2 * t + 2)], inc=c_dve)
                w2 = S.add("vector", lambda v, sl6=sl6: v.reciprocal(
                    rsx[:, sl6:sl6 + 1], ssum[:, sl6:sl6 + 1]),
                    waits=[(c_dve, w1)], inc=c_dve)
                S.add("vector", lambda v, sl6=sl6: v.tensor_scalar(
                    att[:, 2048 * sl6:2048 * (sl6 + 1)], att[:, 2048 * sl6:2048 * (sl6 + 1)],
                    rsx[:, sl6:sl6 + 1], None, AL.mult),
                    waits=[(c_dve, w2)], inc=s_scale)

            if g >= 6:
                hoist_b1_stats(b, 0, g - 6)
            for t in range(3):
                emit_tile_chain(t)
            if pending_dve_tail is not None:
                pending_rstd = pending_dve_tail()
                pending_dve_tail = None
            for t in range(3, 16):
                emit_tile_chain(t)

            def make_tail(g=g):
                def tail():
                    avdone = av_base + 16 * g + 16
                    for dlt in range(2):
                        S.add("vector", lambda v, dlt=dlt: v.bn_stats(
                            bn2[:, 12 * dlt:12 * dlt + 6], Op[dlt][:, 0:512]),
                            waits=[(s_av, avdone)] if dlt == 0 else [])
                        last = S.add("vector", lambda v, dlt=dlt: v.bn_stats(
                            bn2[:, 12 * dlt + 6:12 * dlt + 12], Op[dlt][:, 512:1024]),
                            inc=c_dve)
                    for dlt in range(2):
                        last = S.add("vector", lambda v, dlt=dlt: v.bn_aggr(
                            mv2[:, 2 * dlt:2 * dlt + 2], bn2[:, 12 * dlt:12 * dlt + 12]),
                            waits=[(c_dve, last)] if dlt == 0 else [], inc=c_dve)
                    var2 = bass.AP(mv2, 1, [[4, 128], [2, 2]])
                    mean2 = bass.AP(mv2, 0, [[4, 128], [2, 2]])
                    v2d = S.add("vector", lambda v: v.tensor_scalar(
                        vfix2[:, :], var2, EPS, None, AL.add),
                        waits=[(c_dve, last)], inc=c_dve)
                    negm_done = S.add("vector", lambda v: v.tensor_scalar(
                        nmr2[:, :], mean2, -1.0, None, AL.mult),
                        waits=[(c_dve, v2d)], inc=c_dve)
                    e2_pred = S.cnt[c_act] + 2
                    for dlt in range(2):
                        S.add("vector", lambda v, dlt=dlt, g=g: v.tensor_scalar(
                            u_st[:, 2048 * (g % 2) + 1024 * dlt:2048 * (g % 2) + 1024 * (dlt + 1)],
                            Op[dlt][:, :], nmr2[:, dlt:dlt + 1], rstd2[:, dlt:dlt + 1],
                            AL.add, AL.mult),
                            waits=[(c_act, e2_pred), (c_dve, negm_done),
                                   (s_xadd, xadd_base + 2 * (g - 2) + dlt + 1)],
                            inc=s_apply)
                    return (v2d, e2_pred)
                return tail

            pending_dve_tail = make_tail()

            for k in range(32):
                if k == 10 and pending_rstd is not None:
                    emit_rstd(pending_rstd)
                    pending_rstd = None
                K = 32 * g + k
                t, c = k // 2, k % 2
                sl6 = (16 * g + t) % 8
                att_out = bass.AP(att, 2048 * sl6 + 256 * c,
                                  [[8 * 2048, 128], [512, 4], [1, 256]])
                S.add("scalar", lambda sc, K=K, att_out=att_out, sl6=sl6, c=c: sc.activation(
                    att_out,
                    Sp[K % 2][:, :], AF.Exp, scale=0.125,
                    accum_out=accx[:, 2 * sl6 + c:2 * sl6 + c + 1]),
                    waits=[(s_mm, mm_base + K + 1),
                           (s_av, av_base + 16 * g + t - 7)],
                    inc=s_exp)

            for dlt in range(2):
                if g == 7:
                    continue
                ttile = 2 * g + dlt
                S.add("gpsimd", lambda p, ttile=ttile, dlt=dlt, g=g: p.tensor_tensor(
                    x_sb[:, D * ttile:D * (ttile + 1)], x_sb[:, D * ttile:D * (ttile + 1)],
                    u_st[:, 2048 * (g % 2) + 1024 * dlt:2048 * (g % 2) + 1024 * (dlt + 1)],
                    AL.add),
                    waits=[(s_apply, apply_base + 2 * g + dlt + 1)], inc=s_xadd)
        pending_rstd = pending_dve_tail()
        pending_dve_tail = None
        emit_rstd(pending_rstd)
        pending_rstd = None
        for dlt in range(2):
            ttile = 14 + dlt
            S.add("vector", lambda v, ttile=ttile, dlt=dlt: v.tensor_tensor(
                x_sb[:, D * ttile:D * (ttile + 1)], x_sb[:, D * ttile:D * (ttile + 1)],
                u_st[:, 2048 + 1024 * dlt:2048 + 1024 * (dlt + 1)],
                AL.add),
                waits=[(s_apply, apply_base + 14 + dlt + 1)], inc=s_xaddd)
        if debug_dump and b == 0:
            S.add("gpsimd", lambda g: g.dma_start(out_hbm[1, 768:896, :], x_sb[:, 0:D]),
                  waits=[(s_xadd, S.cnt[s_xadd])], inc=q_st[1], amt=16)
            S.add("gpsimd", lambda g: g.dma_start(out_hbm[1, 896:1024, :], u_st[:, 0:D]),
                  inc=q_st[1], amt=16)
            S.add("gpsimd", lambda g: g.dma_start(out_hbm[1, 1024:1152, :], att[:, 0:D]),
                  inc=q_st[1], amt=16)
            S.add("gpsimd", lambda g: g.dma_start(out_hbm[1, 1152:1280, :], att[:, 2048:2048 + D]),
                  inc=q_st[1], amt=16)
            S.add("sync", None, waits=[(q_st[1], S.cnt[q_st[1]])])
        S.flush(f"a2_{b}")
        a2.close()
        bs.close()

        fb = ExitStack()
        wdT = sb(f"wdT_{b}", [128, NHT * 1024], BF16, fb)

        for H in range(2):
            fh = ExitStack()
            xn2_bf = xn_stage
            hT = sb(f"hT_{b}_{H}", [128, NHT * 1024], BF16, fh)
            wds = ExitStack()

            inline_tp = (b == 0 and H == 0)
            p1 = ExitStack()
            x2T = sb(f"x2T_{b}_{H}", [128, 8 * 1024], BF16, p1)
            stripeW = D if inline_tp else 2 * D
            wgT_s = [sb(f"wgT{i}_{b}_{H}", [128, stripeW], BF16, p1) for i in range(2)]
            wuT_s = [sb(f"wuT{i}_{b}_{H}", [128, stripeW], BF16, p1) for i in range(2)]
            if inline_tp:
                wg_st = [sb(f"wgst{i}_{b}", [128, D], BF16, p1) for i in range(2)]
                wu_st = [sb(f"wust{i}_{b}", [128, D], BF16, p1) for i in range(2)]
                psB = [pt(f"psB{i}_{b}", [128, 1024], BF16, p1) for i in range(4)]
                ps_t2 = psB[:2]
                psTw = psB[:2]
                psTu = psB[2:]
            else:
                ps_t2 = [pt(f"pst2{i}_{b}_{H}", [128, 1024], BF16, p1) for i in range(2)]
            NPS = 2 if inline_tp else 3
            psG = [pt(f"psG{i}_{b}_{H}", [128, 512], F32, p1) for i in range(NPS)]
            psU = [pt(f"psU{i}_{b}_{H}", [128, 512], F32, p1) for i in range(NPS)]

            toks = [8 * H + j for j in range(8)]
            stats_last = {}

            def emit_bstats(hf):
                last = None
                for j in range(4 * hf, 4 * hf + 4):
                    t = toks[j]
                    S.add("vector", lambda v, j=j, t=t: v.bn_stats(
                        bn1[:, 12 * j:12 * j + 6], x_sb[:, D * t:D * t + 512]))
                    last = S.add("vector", lambda v, j=j, t=t: v.bn_stats(
                        bn1[:, 12 * j + 6:12 * j + 12], x_sb[:, D * t + 512:D * t + 1024]),
                        inc=c_dve)
                for j in range(4 * hf, 4 * hf + 4):
                    last = S.add("vector", lambda v, j=j: v.bn_aggr(
                        mvs[:, 2 * j:2 * j + 2], bn1[:, 12 * j:12 * j + 12]),
                        waits=[(c_dve, last)] if j == 4 * hf else [], inc=c_dve)
                stats_last[hf] = last

            nm_done = {}

            def emit_brstd(hf):
                var_v = bass.AP(mvs, 8 * hf + 1, [[2 * NT, 128], [2, 4]])
                mean_v = bass.AP(mvs, 8 * hf, [[2 * NT, 128], [2, 4]])
                c4 = slice(4 * hf, 4 * hf + 4)
                vd = S.add("vector", lambda v, c4=c4, var_v=var_v: v.tensor_scalar(
                    vfix1[:, c4], var_v, EPS, None, AL.add),
                    waits=[(c_dve, stats_last[hf])], inc=c_dve)
                ld2 = S.add("scalar", lambda sc, c4=c4: sc.activation(
                    rstd1[:, c4], vfix1[:, c4], AF.Ln),
                    waits=[(c_dve, vd)], inc=c_act)
                ed2 = S.add("scalar", lambda sc, c4=c4: sc.activation(
                    rstd1[:, c4], rstd1[:, c4], AF.Exp, scale=-0.5),
                    waits=[(c_act, ld2)], inc=c_act)
                nm1 = S.add("vector", lambda v, c4=c4, mean_v=mean_v: v.tensor_tensor(
                    nmr1[:, c4], mean_v, rstd1[:, c4], AL.mult),
                    waits=[(c_act, ed2)], inc=c_dve)
                nm_done[hf] = S.add("vector", lambda v, c4=c4: v.tensor_scalar(
                    nmr1[:, c4], nmr1[:, c4], -1.0, None, AL.mult),
                    waits=[(c_dve, nm1)], inc=c_dve)

            tpb = S.cnt[s_tp]
            xnb = S.cnt[s_xn]
            ev_mark = {-2: None, -1: None}

            def emit_bapply(j):
                t = toks[j]
                S.add("scalar", lambda sc, j=j, t=t: sc.activation(
                    xn2_bf[:, D * (j % 2):D * (j % 2 + 1)], x_sb[:, D * t:D * (t + 1)],
                    AF.Identity, bias=nmr1[:, j:j + 1], scale=rstd1[:, j:j + 1]),
                    waits=[(c_dve, nm_done[j // 4]), (s_tp, tpb + j - 1)], inc=s_xn)
                for dt in range(8):
                    w = []
                    if dt == 0:
                        w = [(s_xn, xnb + j + 1)]
                        if ev_mark[j - 2] is not None:
                            w.append(ev_mark[j - 2])
                    S.add("tensor", lambda tt, j=j, dt=dt: tt.transpose(
                        ps_t2[j % 2][:, 128 * dt:128 * (dt + 1)],
                        xn2_bf[:, D * (j % 2) + 128 * dt:D * (j % 2) + 128 * (dt + 1)],
                        ident[:, :]),
                        waits=w, inc=(s_tp if dt == 7 else None))

            def emit_bevac(j):
                dest = bass.AP(x2T, 128 * j, [[8 * 1024, 128], [1024, 8], [1, 128]])
                if j % 2 == 0:
                    c = S.add("scalar", lambda sc, dest=dest, j=j: sc.copy(
                        dest, ps_t2[j % 2][:, :]),
                        waits=[(s_tp, tpb + j + 1)], inc=s_evT)
                    ev_mark[j] = (s_evT, c)
                else:
                    c = S.add("vector", lambda v, dest=dest, j=j: v.tensor_copy(
                        dest, ps_t2[j % 2][:, :]),
                        waits=[(s_tp, tpb + j + 1)], inc=s_evD)
                    ev_mark[j] = (s_evD, c)

            stats_last[0] = b1_stats_last[(b, H, 0)]
            stats_last[1] = b1_stats_last[(b, H, 1)]
            emit_brstd(0)
            for j in range(4):
                emit_bapply(j)
                if j > 0:
                    emit_bevac(j - 1)
            emit_brstd(1)
            for j in range(4, 8):
                emit_bapply(j)
                emit_bevac(j - 1)
            emit_bevac(7)
            x2T_c0 = [ev_mark[2], ev_mark[3]]
            x2T_w = [ev_mark[6], ev_mark[7]]

            gup_base = S.cnt[s_gup]
            silu_base = S.cnt[s_silu]
            hT_base = S.cnt[s_hT]
            ldT_g = {}
            ldT_u = {}

            if inline_tp:
                wT_base = S.cnt[s_wT]
                wgev_base = S.cnt[s_wgev]
                wuev_base = S.cnt[s_wuev]
                wgw_base = [S.cnt[q_wgw[0]], S.cnt[q_wgw[1]]]
                wuw_base = [S.cnt[q_wuw[0]], S.cnt[q_wuw[1]]]

                def emit_wdma(ht):
                    sl = ht % 2
                    ldT_g[ht] = S.add("gpsimd", lambda g, ht=ht, sl=sl: g.dma_start(
                        wg_st[sl][:, :], wg_hbm[128 * ht:128 * (ht + 1), :]),
                        waits=[(s_wT, wT_base + 2 * (ht - 2) + 1)], inc=q_wgi[sl], amt=16)
                    ldT_u[ht] = S.add("gpsimd", lambda g, ht=ht, sl=sl: g.dma_start(
                        wu_st[sl][:, :], wu_hbm[128 * ht:128 * (ht + 1), :]),
                        waits=[(s_wT, wT_base + 2 * (ht - 2) + 2)], inc=q_wui[sl], amt=16)

                def emit_wtp(ht):
                    sl = ht % 2
                    for dt in range(8):
                        S.add("tensor", lambda tt, sl=sl, dt=dt: tt.transpose(
                            psTw[sl][:, 128 * dt:128 * (dt + 1)],
                            wg_st[sl][:, 128 * dt:128 * (dt + 1)], ident[:, :]),
                            waits=([(q_wgi[sl], ldT_g[ht]), (s_wgev, wgev_base + (ht - 2) + 1)]
                                   + x2T_w if dt == 0 else []),
                            inc=(s_wT if dt == 7 else None))
                    for dt in range(8):
                        S.add("tensor", lambda tt, sl=sl, dt=dt: tt.transpose(
                            psTu[sl][:, 128 * dt:128 * (dt + 1)],
                            wu_st[sl][:, 128 * dt:128 * (dt + 1)], ident[:, :]),
                            waits=([(q_wui[sl], ldT_u[ht]), (s_wuev, wuev_base + (ht - 2) + 1)]
                                   if dt == 0 else []),
                            inc=(s_wT if dt == 7 else None))

                def emit_wev(ht):
                    sl = ht % 2
                    evg = S.add("scalar", lambda sc, sl=sl: sc.copy(wgT_s[sl][:, :], psTw[sl][:, :]),
                                waits=[(s_wT, wT_base + 2 * ht + 1),
                                       (q_wgw[sl], wgw_base[sl] + 16 * ((ht - 2) // 2 + 1)
                                        if ht >= 2 else 0)],
                                inc=s_wgev)
                    S.add("sync", lambda s, ht=ht, sl=sl: s.dma_start(
                        wgT_dram[ht, :, :], wgT_s[sl][:, :]),
                        waits=[(s_wgev, evg)], inc=q_wgw[sl], amt=16)
                    evu = S.add("vector", lambda v, sl=sl: v.tensor_copy(wuT_s[sl][:, :], psTu[sl][:, :]),
                                waits=[(s_wT, wT_base + 2 * ht + 2),
                                       (q_wuw[sl], wuw_base[sl] + 16 * ((ht - 2) // 2 + 1)
                                        if ht >= 2 else 0)],
                                inc=s_wuev)
                    S.add("sync", lambda s, ht=ht, sl=sl: s.dma_start(
                        wuT_dram[ht, :, :], wuT_s[sl][:, :]),
                        waits=[(s_wuev, evu)], inc=q_wuw[sl], amt=16)

                emit_wdma(0)
                emit_wdma(1)
                emit_wtp(0)
                emit_wev(0)
                emit_wtp(1)
                emit_wev(1)
            else:
                def emit_pair_load(p):
                    sl = p % 2
                    wgr = [(s_gup, max(gup_base + 8 * p - 9, 0))]
                    wur = [(s_gup, max(gup_base + 8 * p - 8, 0))]
                    if p < 2 and b == 0:
                        wgr += [(q_wgw[0], S.cnt[q_wgw[0]]), (q_wgw[1], S.cnt[q_wgw[1]])]
                        wur += [(q_wuw[0], S.cnt[q_wuw[0]]), (q_wuw[1], S.cnt[q_wuw[1]])]
                    srcg = bass.AP(wgT_dram.tensor, 2 * p * 128 * 1024,
                                   [[1024, 128], [128 * 1024, 2], [1, 1024]])
                    srcu = bass.AP(wuT_dram.tensor, 2 * p * 128 * 1024,
                                   [[1024, 128], [128 * 1024, 2], [1, 1024]])
                    ldT_g[p] = S.add("sync", lambda s, sl=sl, srcg=srcg: s.dma_start(
                        wgT_s[sl][:, :], srcg),
                        waits=wgr, inc=q_wgt[sl], amt=16)
                    ldT_u[p] = S.add("sync", lambda s, sl=sl, srcu=srcu: s.dma_start(
                        wuT_s[sl][:, :], srcu),
                        waits=wur, inc=q_wut[sl], amt=16)

                emit_pair_load(0)
                emit_pair_load(1)
            if H == 0:
                wdld = bass.AP(wdT_dram.tensor, 0,
                               [[1024, 128], [128 * 1024, NHT], [1, 1024]])
                wdst = bass.AP(wdT, 0, [[NHT * 1024, 128], [1024, NHT], [1, 1024]])
                S.add("sync", lambda s, wdst=wdst, wdld=wdld: s.dma_start(wdst, wdld),
                      waits=[(q_wdw[i], S.cnt[q_wdw[i]]) for i in range(4)],
                      inc=q_wdT, amt=16)
            for ht in range(NHT):
                sl = ht % 2
                if inline_tp:
                    wsl, wcol = sl, 0
                    if ht + 2 < NHT:
                        emit_wdma(ht + 2)
                else:
                    wsl, wcol = (ht // 2) % 2, D * (ht % 2)
                    if ht % 2 == 0 and ht // 2 + 2 < NHT // 2:
                        emit_pair_load(ht // 2 + 2)
                for c2 in range(2):
                    ci = 2 * ht + c2
                    gsl = ci % NPS
                    for dt in range(8):
                        waits = []
                        if dt == 0:
                            waits = ([(s_wgev, wgev_base + ht + 1)] if inline_tp
                                     else [(q_wgt[wsl], ldT_g[ht // 2])]) + [
                                     (s_silu, silu_base + ci - NPS + 1)] + (
                                     x2T_c0 if c2 == 0 else x2T_w)
                        S.add("tensor", lambda tt, wsl=wsl, wcol=wcol, dt=dt, gsl=gsl, c2=c2: tt.matmul(
                            psG[gsl][:, :], wgT_s[wsl][:, wcol + 128 * dt:wcol + 128 * (dt + 1)],
                            x2T[:, 1024 * dt + 512 * c2:1024 * dt + 512 * (c2 + 1)],
                            start=(dt == 0), stop=(dt == 7)),
                            waits=waits, inc=(s_gup if dt == 7 else None))
                    for dt in range(8):
                        waits = []
                        if dt == 0:
                            waits = ([(s_wuev, wuev_base + ht + 1)] if inline_tp
                                     else [(q_wut[wsl], ldT_u[ht // 2])]) + [
                                     (s_hT, hT_base + ci - NPS + 1)]
                        S.add("tensor", lambda tt, wsl=wsl, wcol=wcol, dt=dt, gsl=gsl, c2=c2: tt.matmul(
                            psU[gsl][:, :], wuT_s[wsl][:, wcol + 128 * dt:wcol + 128 * (dt + 1)],
                            x2T[:, 1024 * dt + 512 * c2:1024 * dt + 512 * (c2 + 1)],
                            start=(dt == 0), stop=(dt == 7)),
                            waits=waits, inc=(s_gup if dt == 7 else None))
                for c2 in range(2):
                    ci = 2 * ht + c2
                    gsl = ci % NPS
                    dstH = hT[:, 1024 * ht + 512 * c2:1024 * ht + 512 * (c2 + 1)]
                    S.add("scalar", lambda sc, gsl=gsl, dstH=dstH: sc.activation(
                        dstH, psG[gsl][:, :], AF.Silu, scale=1.0 / 127.0),
                        waits=[(s_gup, gup_base + 4 * ht + 2 * c2 + 1)], inc=s_silu)
                    S.add("vector", lambda v, gsl=gsl, dstH=dstH: v.tensor_tensor(
                        dstH, dstH, psU[gsl][:, :], AL.mult),
                        waits=[(s_silu, silu_base + ci + 1),
                               (s_gup, gup_base + 4 * ht + 2 * c2 + 2)], inc=s_hT)
                if inline_tp and ht + 2 < NHT:
                    emit_wtp(ht + 2)
                    emit_wev(ht + 2)
            S.flush(f"b1_{b}_{H}")
            p1.close()

            p2 = ExitStack()
            u2 = sb(f"u2_{b}_{H}", [128, 2 * 2048], F32, p2)
            psD = [[pt(f"psD{i}{tau}_{b}_{H}", [128, 1024], F32, p2) for tau in range(2)]
                   for i in range(2)]
            down_base = S.cnt[s_down]
            ap3_base = S.cnt[s_ap3]
            xadd3_base = S.cnt[s_xadd3]
            hT_total = S.cnt[s_hT]
            wdT_ready = (q_wdT, 16 * (b + 1))
            add_d = {}
            for Sc in range(4):
                slot = Sc % 2
                for ht in range(NHT):
                    for tau in range(2):
                        for d2h in range(2):
                            waits = []
                            if ht == 0 and tau == 0 and d2h == 0:
                                waits = [(s_hT, hT_total), wdT_ready,
                                         (s_ap3, ap3_base + 2 * (Sc - 2) + 2)]
                            S.add("tensor", lambda tt, slot=slot, tau=tau, d2h=d2h, ht=ht, Sc=Sc: tt.matmul(
                                psD[slot][tau][:, 512 * d2h:512 * (d2h + 1)],
                                hT[:, 1024 * ht + 256 * Sc + 128 * tau:
                                   1024 * ht + 256 * Sc + 128 * (tau + 1)],
                                wdT[:, 1024 * ht + 512 * d2h:1024 * ht + 512 * (d2h + 1)],
                                start=(ht == 0), stop=(ht == NHT - 1)),
                                waits=waits,
                                inc=(s_down if (ht == NHT - 1 and tau == 1 and d2h == 1) else None))
                dn = S.cnt[s_down]
                for tau in range(2):
                    S.add("vector", lambda v, slot=slot, tau=tau: v.bn_stats(
                        bn2[:, 12 * tau:12 * tau + 6], psD[slot][tau][:, 0:512]),
                        waits=[(s_down, dn),
                               (s_ap3, ap3_base + 2 * (Sc - 1) + 2)] if tau == 0 else [])
                    last3 = S.add("vector", lambda v, slot=slot, tau=tau: v.bn_stats(
                        bn2[:, 12 * tau + 6:12 * tau + 12], psD[slot][tau][:, 512:1024]),
                        inc=c_dve)
                for tau in range(2):
                    last3 = S.add("vector", lambda v, tau=tau: v.bn_aggr(
                        mv2[:, 2 * tau:2 * tau + 2], bn2[:, 12 * tau:12 * tau + 12]),
                        waits=[(c_dve, last3)] if tau == 0 else [], inc=c_dve)
                var3 = bass.AP(mv2, 1, [[4, 128], [2, 2]])
                mean3 = bass.AP(mv2, 0, [[4, 128], [2, 2]])
                v3 = S.add("vector", lambda v: v.tensor_scalar(
                    vfix2[:, :], var3, EPS3, None, AL.add),
                    waits=[(c_dve, last3)], inc=c_dve)
                l3 = S.add("scalar", lambda sc: sc.activation(rstd2[:, :], vfix2[:, :], AF.Ln),
                           waits=[(c_dve, v3)], inc=c_act)
                e3 = S.add("scalar", lambda sc: sc.activation(rstd2[:, :], rstd2[:, :], AF.Exp, scale=-0.5),
                           waits=[(c_act, l3)], inc=c_act)
                n31 = S.add("vector", lambda v: v.tensor_tensor(nmr2[:, :], mean3, rstd2[:, :], AL.mult),
                            waits=[(c_act, e3)], inc=c_dve)
                n32 = S.add("vector", lambda v: v.tensor_scalar(nmr2[:, :], nmr2[:, :], -1.0, None, AL.mult),
                            waits=[(c_dve, n31)], inc=c_dve)
                if H == 0 and Sc in (1, 2):
                    hoist_b1_stats(b, 1, Sc - 1)
                if b == 0 and H == 1 and Sc in (1, 2):
                    hoist_a1_stats(1, Sc - 1)
                if b == 0 and H == 1 and Sc == 2:
                    hoist_a1_stats(1, 2)
                for tau in range(2):
                    S.add("scalar", lambda sc, slot=slot, tau=tau, Sc=Sc: sc.activation(
                        u2[:, 2048 * slot + 1024 * tau:2048 * slot + 1024 * (tau + 1)],
                        psD[slot][tau][:, :], AF.Identity,
                        bias=nmr2[:, tau:tau + 1], scale=rstd2[:, tau:tau + 1]),
                        waits=[(c_dve, n32),
                               (s_xadd3, xadd3_base + 2 * (Sc - 2) + tau + 1)],
                        inc=s_ap3)
                for tau in range(2):
                    tt_idx = 8 * H + 2 * Sc + tau
                    if Sc == 3:
                        add_d[tau] = S.add("vector", lambda v, slot=slot, tau=tau, tt_idx=tt_idx: v.tensor_tensor(
                            x_sb[:, D * tt_idx:D * (tt_idx + 1)],
                            x_sb[:, D * tt_idx:D * (tt_idx + 1)],
                            u2[:, 2048 * slot + 1024 * tau:2048 * slot + 1024 * (tau + 1)],
                            AL.add),
                            waits=[(s_ap3, ap3_base + 2 * Sc + tau + 1)], inc=s_xadd3d)
                    else:
                        S.add("gpsimd", lambda p, slot=slot, tau=tau, tt_idx=tt_idx: p.tensor_tensor(
                            x_sb[:, D * tt_idx:D * (tt_idx + 1)],
                            x_sb[:, D * tt_idx:D * (tt_idx + 1)],
                            u2[:, 2048 * slot + 1024 * tau:2048 * slot + 1024 * (tau + 1)],
                            AL.add),
                            waits=[(s_ap3, ap3_base + 2 * Sc + tau + 1)], inc=s_xadd3)
                for tau in range(2):
                    tt_idx = 8 * H + 2 * Sc + tau
                    stw = ([(s_xadd3, xadd3_base + 6), (s_xadd3d, add_d[tau])]
                           if Sc == 3 else
                           [(s_xadd3, xadd3_base + 2 * Sc + tau + 1)])
                    S.add("sync", lambda s, tt_idx=tt_idx, b=b: s.dma_start(
                        out_hbm[b, 128 * tt_idx:128 * (tt_idx + 1), :],
                        x_sb[:, D * tt_idx:D * (tt_idx + 1)]),
                        waits=stw,
                        inc=q_st[tt_idx // 4], amt=16)
                if b == 0 and Sc % 2 == 1:
                    grp = 2 * H + (Sc - 1) // 2
                    xld_counts[1][grp] = emit_xload(1, grp)
            S.flush(f"b2_{b}_{H}")
            p2.close()
            wds.close()
            fh.close()
        fb.close()

    S.add("sync", None, waits=[(q_st[i], S.cnt[q_st[i]]) for i in range(4)])
    S.flush("fin")
    es.close()
    return nc


_CACHE = {}


def kernel(L0_attn_x, L0_attn_Mq, L0_attn_Mk, L0_attn_Mv,
           L0_ffn_w_gate, L0_ffn_w_up, L0_ffn_w_down):
    x = np.ascontiguousarray(np.asarray(L0_attn_x, dtype=np.float32))
    mq = np.ascontiguousarray(np.asarray(L0_attn_Mq, dtype=np.float32))
    mk = np.ascontiguousarray(np.asarray(L0_attn_Mk, dtype=np.float32))
    mv = np.ascontiguousarray(np.asarray(L0_attn_Mv, dtype=np.float32))
    wg = np.ascontiguousarray(np.asarray(L0_ffn_w_gate, dtype=np.int32))
    wu = np.ascontiguousarray(np.asarray(L0_ffn_w_up, dtype=np.int32))
    wd = np.ascontiguousarray(np.asarray(L0_ffn_w_down, dtype=np.int32))

    if "nc" not in _CACHE:
        _CACHE["nc"] = build()
    nc = _CACHE["nc"]

    in_maps = []
    for c in range(8):
        in_maps.append({
            "x": x[NB * c:NB * (c + 1)],
            "mq": mq, "mk": mk, "mv": mv,
            "wg": wg, "wu": wu, "wd": wd,
        })
    res = run_bass_kernel_spmd(nc, in_maps, core_ids=list(range(8)))
    return np.concatenate([r["out"] for r in res.results], axis=0)




# version 36 (final)
# speedup vs baseline: 1.0136x; 1.0024x over previous
import numpy as np
from contextlib import ExitStack

import concourse.bass as bass
import concourse.mybir as mybir
from concourse.bass_utils import run_bass_kernel_spmd
from concourse.alu_op_type import AluOpType

F32 = mybir.dt.float32
BF16 = mybir.dt.bfloat16
I32 = mybir.dt.int32
AF = mybir.ActivationFunctionType
AL = AluOpType

NB = 2
NT = 16
D = 1024
HID = 3072
NHT = 24
EPS = 1e-5
EPS3 = 1e-5 * 16129.0 ** 2


class Sched:

    ENGINES = ("tensor", "scalar", "vector", "gpsimd", "sync")

    def __init__(self, nc, es):
        self.nc = nc
        self.es = es
        self.cnt = {}
        self.h = {}

    def sem(self, name):
        self.h[name] = self.es.enter_context(self.nc.semaphore(name))
        self.cnt[name] = 0
        return name

    def begin(self):
        self.ops = {e: [] for e in self.ENGINES}

    def add(self, eng, fn, waits=(), inc=None, amt=1):
        w = [(s, v) for (s, v) in waits if v > 0]
        self.ops[eng].append((w, fn, inc, amt))
        if inc is not None:
            self.cnt[inc] += amt
            return self.cnt[inc]
        return None

    def flush(self, name=None):
        nc, h = self.nc, self.h
        self.nflush = getattr(self, "nflush", 0) + 1
        if getattr(self, "stop_after", None) is not None and self.nflush > self.stop_after:
            self.begin()
            return
        ops = self.ops
        with nc.Block(name) as block:
            for eng_name in self.ENGINES:
                lst = ops[eng_name]
                if not lst:
                    continue

                def body(eng, lst=lst):
                    for w, fn, inc, amt in lst:
                        for s, v in w:
                            eng.wait_ge(h[s], v)
                        if fn is None:
                            continue
                        instr = fn(eng)
                        if inc is not None:
                            instr.then_inc(h[inc], amt)

                getattr(block, eng_name)(body)
        self.begin()


def build(stop_after=None, debug_dump=False):
    nc = bass.Bass("TRN2", target_bir_lowering=False, dynamic_dma_scratch_size=16384)
    x_hbm = nc.dram_tensor("x", [NB, 2048, D], F32, kind="ExternalInput").ap()
    mq_hbm = nc.dram_tensor("mq", [D, 512], F32, kind="ExternalInput").ap()
    mk_hbm = nc.dram_tensor("mk", [D, 512], F32, kind="ExternalInput").ap()
    mv_hbm = nc.dram_tensor("mv", [D, D], F32, kind="ExternalInput").ap()
    wg_hbm = nc.dram_tensor("wg", [HID, D], I32, kind="ExternalInput").ap()
    wu_hbm = nc.dram_tensor("wu", [HID, D], I32, kind="ExternalInput").ap()
    wd_hbm = nc.dram_tensor("wd", [D, HID], I32, kind="ExternalInput").ap()
    out_hbm = nc.dram_tensor("out", [NB, 2048, D], F32, kind="ExternalOutput").ap()
    wgT_dram = nc.dram_tensor("wgT_scratch", [NHT, 128, 1024], BF16, kind="Internal").ap()
    wuT_dram = nc.dram_tensor("wuT_scratch", [NHT, 128, 1024], BF16, kind="Internal").ap()
    wdT_dram = nc.dram_tensor("wdT_scratch", [NHT, 128, 1024], BF16, kind="Internal").ap()

    es = ExitStack()
    S = Sched(nc, es)
    S.stop_after = stop_after

    def sb(name, shape, dt, scope=es):
        return scope.enter_context(nc.sbuf_tensor(name, shape, dt))

    def pt(name, shape, dt, scope=es):
        return scope.enter_context(nc.psum_tensor(name, shape, dt))

    q_x = [S.sem(f"q_x{i}") for i in range(4)]
    q_x0 = S.sem("q_xe")
    q_st = [S.sem(f"q_st{i}") for i in range(4)]
    q_wq = S.sem("q_wq")
    q_wk = S.sem("q_wk")
    q_wks = S.sem("q_wks")
    q_wv = S.sem("q_wv")
    q_wv1 = S.sem("q_wv1")
    q_ksw = S.sem("q_ksw")
    q_wg = [S.sem(f"q_wg{i}") for i in range(2)]
    q_wu = [S.sem(f"q_wu{i}") for i in range(2)]
    q_wst = [S.sem(f"q_wst{i}") for i in range(4)]
    q_wstd = [S.sem(f"q_wstd{i}") for i in range(2)]
    q_wgw = [S.sem(f"q_wgw{i}") for i in range(2)]
    q_wuw = [S.sem(f"q_wuw{i}") for i in range(2)]
    q_wtr = [S.sem(f"q_wtr{i}") for i in range(2)]
    q_wgt = [S.sem(f"q_wgt{i}") for i in range(2)]
    q_wut = [S.sem(f"q_wut{i}") for i in range(2)]
    q_wgi = [S.sem(f"q_wgi{i}") for i in range(2)]
    q_wui = [S.sem(f"q_wui{i}") for i in range(2)]
    q_wdi = [S.sem(f"q_wdi{i}") for i in range(2)]
    q_wdw = [S.sem(f"q_wdw{i}") for i in range(4)]
    q_wdT = S.sem("q_wdT")
    c_act = S.sem("c_act")
    c_dve = S.sem("c_dve")
    c_pool = S.sem("c_pool")
    s_xn = S.sem("s_xn")
    s_tp = S.sem("s_tp")
    s_evT = S.sem("s_evT")
    s_evD = S.sem("s_evD")
    s_proj = S.sem("s_proj")
    s_evP = S.sem("s_evP")
    s_evPA = S.sem("s_evPA")
    s_mm = S.sem("s_mm")
    s_exp = S.sem("s_exp")
    s_scale = S.sem("s_scale")
    s_av = S.sem("s_av")
    s_apply = S.sem("s_apply")
    s_xadd = S.sem("s_xadd")
    s_wT = S.sem("s_wT")
    s_wgev = S.sem("s_wgev")
    s_wuev = S.sem("s_wuev")
    s_wdev = S.sem("s_wdev")
    s_gup = S.sem("s_gup")
    s_silu = S.sem("s_silu")
    s_hT = S.sem("s_hT")
    s_down = S.sem("s_down")
    s_ap3 = S.sem("s_ap3")
    s_xadd3 = S.sem("s_xadd3")
    s_xadd3d = S.sem("s_xadd3d")
    s_xaddd = S.sem("s_xaddd")

    S.begin()

    ident = sb("ident", [128, 128], BF16)
    setup_es = ExitStack()
    io_a = sb("io_a", [128, 128], F32, setup_es)
    io_b = sb("io_b", [128, 128], F32, setup_es)
    S.add("gpsimd", lambda g: g.iota(io_a[:, :], [[1, 128]], channel_multiplier=0,
                                     allow_small_or_imprecise_dtypes=True), inc=c_pool)
    S.add("gpsimd", lambda g: g.iota(io_b[:, :], [[0, 128]], channel_multiplier=1,
                                     allow_small_or_imprecise_dtypes=True), inc=c_pool)
    S.add("vector", lambda v: v.tensor_tensor(ident[:, :], io_a[:, :], io_b[:, :], AL.is_equal),
          waits=[(c_pool, 2)], inc=c_dve)
    S.flush("setup")
    setup_es.close()

    x_sb = sb("x_sb", [128, NT * D], F32)
    xn_stage = sb("xn_stage", [128, 2 * D], BF16)
    bn1 = sb("bn1", [128, 12 * NT], F32)
    mvs = sb("mvs", [128, 2 * NT], F32)
    vfix1 = sb("vfix1", [128, NT], F32)
    rstd1 = sb("rstd1", [128, NT], F32)
    nmr1 = sb("nmr1", [128, NT], F32)
    accx = sb("accx", [128, 16], F32)
    ssum = sb("ssum", [128, 8], F32)
    rsx = sb("rsx", [128, 8], F32)
    bn2 = sb("bn2", [128, 24], F32)
    mv2 = sb("mv2", [128, 4], F32)
    vfix2 = sb("vfix2", [128, 2], F32)
    rstd2 = sb("rstd2", [128, 2], F32)
    nmr2 = sb("nmr2", [128, 2], F32)

    xld_counts = {0: {}, 1: {}}
    b1_stats_last = {}
    a2_add_bases = {}

    a1_stats_last = {}

    def hoist_a1_stats(bb, g4):
        last = None
        for t in range(4 * g4, 4 * g4 + 4):
            S.add("vector", lambda v, t=t: v.bn_stats(
                bn1[:, 12 * t:12 * t + 6], x_sb[:, D * t:D * t + 512]),
                waits=[(q_x[t // 4], xld_counts[bb][t // 4])])
            last = S.add("vector", lambda v, t=t: v.bn_stats(
                bn1[:, 12 * t + 6:12 * t + 12], x_sb[:, D * t + 512:D * t + 1024]),
                inc=c_dve)
        for t in range(4 * g4, 4 * g4 + 4):
            last = S.add("vector", lambda v, t=t: v.bn_aggr(
                mvs[:, 2 * t:2 * t + 2], bn1[:, 12 * t:12 * t + 12]),
                waits=[(c_dve, last)] if t == 4 * g4 else [], inc=c_dve)
        a1_stats_last[(bb, g4)] = last

    def hoist_b1_stats(bb, HH, hf):
        xab, xdb = a2_add_bases[bb]
        last = None
        for j in range(4 * hf, 4 * hf + 4):
            t = 8 * HH + j
            xw = (s_xadd, xab + t + 1) if t <= 13 else (s_xaddd, xdb + t - 13)
            S.add("vector", lambda v, j=j, t=t: v.bn_stats(
                bn1[:, 12 * j:12 * j + 6], x_sb[:, D * t:D * t + 512]),
                waits=[xw])
            last = S.add("vector", lambda v, j=j, t=t: v.bn_stats(
                bn1[:, 12 * j + 6:12 * j + 12], x_sb[:, D * t + 512:D * t + 1024]),
                inc=c_dve)
        for j in range(4 * hf, 4 * hf + 4):
            last = S.add("vector", lambda v, j=j: v.bn_aggr(
                mvs[:, 2 * j:2 * j + 2], bn1[:, 12 * j:12 * j + 12]),
                waits=[(c_dve, last)] if j == 4 * hf else [], inc=c_dve)
        b1_stats_last[(bb, HH, hf)] = last

    def emit_xload(bb, i):
        dst = bass.AP(x_sb, 4096 * i, [[NT * D, 128], [D, 4], [1, D]])
        srcx = bass.AP(x_hbm.tensor, bb * 2048 * D + 512 * i * D,
                       [[D, 128], [128 * D, 4], [1, D]])
        w = [(q_st[i], S.cnt[q_st[i]])] if bb > 0 else []
        return S.add("sync", lambda s, dst=dst, srcx=srcx: s.dma_start(dst, srcx),
                     waits=w, inc=q_x[i], amt=16)

    for b in range(NB):
        bs = ExitStack()

        qT = sb(f"qT_{b}", [128, 4 * 2048], BF16, bs)
        kT = sb(f"kT_{b}", [128, 4 * 2048], BF16, bs)
        kTs = sb(f"kTs_{b}", [128, 4 * 2048], BF16, bs)
        v_sb = sb(f"v_{b}", [128, NT * D], BF16, bs)

        a1 = ExitStack()
        xn_bf = xn_stage
        xnT = sb(f"xnT_{b}", [128, 8 * 2048], BF16, a1)
        w_pg = [sb(f"wpg{i}_{b}", [128, 8 * 512], BF16, a1) for i in range(2)]
        wks_sb = sb(f"wks_{b}", [128, 8 * 512], BF16, a1)
        ps_tp = [pt(f"pstp{i}_{b}", [128, 1024], BF16, a1) for i in range(2)]
        psP = [pt(f"psP{i}_{b}", [128, 512], F32, a1) for i in range(4)]

        if b == 1:
            for e in ("sync", "gpsimd", "vector", "scalar"):
                S.add(e, None, waits=[(q_wdT, S.cnt[q_wdT])])
        if b == 0:
            S.add("sync", lambda s: s.dma_start(
                bass.AP(x_sb, 0, [[NT * D, 128], [D, 2], [1, D]]),
                bass.AP(x_hbm.tensor, 0, [[D, 128], [128 * D, 2], [1, D]])),
                inc=q_x0, amt=16)
            xld_counts[0][0] = S.add("sync", lambda s: s.dma_start(
                bass.AP(x_sb, 2048, [[NT * D, 128], [D, 2], [1, D]]),
                bass.AP(x_hbm.tensor, 2 * 128 * D, [[D, 128], [128 * D, 2], [1, D]])),
                inc=q_x[0], amt=16)
            for i in range(1, 4):
                xld_counts[0][i] = emit_xload(0, i)
        xld = xld_counts[b]
        dst_q = bass.AP(w_pg[0], 0, [[4096, 128], [512, 8], [1, 512]])
        src_q = bass.AP(mq_hbm.tensor, 0, [[512, 128], [65536, 8], [1, 512]])
        wq_ld = S.add("gpsimd", lambda g: g.dma_start(dst_q, src_q), inc=q_wq, amt=16)
        dst_k = bass.AP(w_pg[1], 0, [[4096, 128], [512, 8], [1, 512]])
        src_k = bass.AP(mk_hbm.tensor, 0, [[512, 128], [65536, 8], [1, 512]])
        wk_ld = S.add("gpsimd", lambda g: g.dma_start(dst_k, src_k), inc=q_wk, amt=16)
        dstv1 = bass.AP(kTs, 0, [[8192, 128], [1024, 4], [1, 1024]])
        srcv1 = bass.AP(mv_hbm.tensor, 4 * 128 * D,
                        [[D, 128], [128 * D, 4], [1, D]])
        wv1_early = S.add("gpsimd", lambda g, dstv1=dstv1, srcv1=srcv1: g.dma_start(
            dstv1, srcv1), inc=q_wv1, amt=16)

        tp_base = S.cnt[s_tp]
        evT_base = S.cnt[s_evT]
        xn_base = S.cnt[s_xn]
        for g4 in range(4):
            tiles = [4 * g4 + j for j in range(4)]
            if (b, g4) in a1_stats_last:
                aggr_done = a1_stats_last[(b, g4)]
            else:
                for t in tiles:
                    xw = (q_x0, 16) if (b == 0 and t < 2) else (q_x[t // 4], xld[t // 4])
                    S.add("vector", lambda v, t=t: v.bn_stats(bn1[:, 12 * t:12 * t + 6],
                                                              x_sb[:, D * t:D * t + 512]),
                          waits=[xw])
                    stats_done = S.add("vector", lambda v, t=t: v.bn_stats(
                        bn1[:, 12 * t + 6:12 * t + 12], x_sb[:, D * t + 512:D * t + 1024]),
                        inc=c_dve)
                for t in tiles:
                    aggr_done = S.add("vector", lambda v, t=t: v.bn_aggr(
                        mvs[:, 2 * t:2 * t + 2], bn1[:, 12 * t:12 * t + 12]),
                        waits=[(c_dve, stats_done)] if t == tiles[0] else [], inc=c_dve)
            var_v4 = bass.AP(mvs, 8 * g4 + 1, [[2 * NT, 128], [2, 4]])
            mean_v4 = bass.AP(mvs, 8 * g4, [[2 * NT, 128], [2, 4]])
            veps_done = S.add("vector", lambda v, g4=g4, var_v4=var_v4: v.tensor_scalar(
                vfix1[:, 4 * g4:4 * g4 + 4], var_v4, EPS, None, AL.add),
                waits=[(c_dve, aggr_done)], inc=c_dve)
            ln_done = S.add("scalar", lambda sc, g4=g4: sc.activation(
                rstd1[:, 4 * g4:4 * g4 + 4], vfix1[:, 4 * g4:4 * g4 + 4], AF.Ln),
                waits=[(c_dve, veps_done)], inc=c_act)
            rstd_done = S.add("scalar", lambda sc, g4=g4: sc.activation(
                rstd1[:, 4 * g4:4 * g4 + 4], rstd1[:, 4 * g4:4 * g4 + 4], AF.Exp, scale=-0.5),
                waits=[(c_act, ln_done)], inc=c_act)
            negm_done = S.add("vector", lambda v, g4=g4, mean_v4=mean_v4: v.tensor_scalar(
                nmr1[:, 4 * g4:4 * g4 + 4], mean_v4, -1.0, None, AL.mult),
                waits=[(c_act, rstd_done)], inc=c_dve)
            for t in tiles:
                S.add("vector", lambda v, t=t: v.tensor_scalar(
                    xn_bf[:, D * (t % 2):D * (t % 2 + 1)], x_sb[:, D * t:D * (t + 1)],
                    nmr1[:, t:t + 1], rstd1[:, t:t + 1], AL.add, AL.mult),
                    waits=[(c_dve, negm_done), (s_tp, tp_base + t - 1)], inc=s_xn)
                for dt in range(8):
                    S.add("tensor", lambda tt, t=t, dt=dt: tt.transpose(
                        ps_tp[t % 2][:, 128 * dt:128 * (dt + 1)],
                        xn_bf[:, D * (t % 2) + 128 * dt:D * (t % 2) + 128 * (dt + 1)],
                        ident[:, :]),
                        waits=([(s_xn, xn_base + t + 1), (s_evT, evT_base + t - 1)]
                               if dt == 0 else []),
                        inc=(s_tp if dt == 7 else None))
                dest = bass.AP(xnT, 128 * t, [[8 * 2048, 128], [2048, 8], [1, 128]])
                S.add("scalar", lambda sc, t=t, dest=dest: sc.copy(dest, ps_tp[t % 2][:, :]),
                      waits=[(s_tp, tp_base + t + 1)], inc=s_evT)

        proj_groups = []
        for q in range(4):
            for c in range(4):
                proj_groups.append(("q", q, c))
        for q in range(4):
            for c in range(4):
                proj_groups.append(("k", q, c))
        for t in range(NT):
            for c2 in range(2):
                proj_groups.append(("v", t, c2))

        wv_ld = None
        proj_base = S.cnt[s_proj]
        evP_base = S.cnt[s_evP]
        evPA_base = S.cnt[s_evPA]

        if b == 0:
            wd_ld = {}
            wdev0 = S.cnt[s_wdev]
            wdtp0 = S.cnt[s_wT]
            wd_emitted = [0]

            def wd_stage_dma(u):
                d2t, grp = u // 3, u % 3
                w = [(s_tp, tp_base + NT)] if u < 2 else [(s_wT, wdtp0 + u - 1)]
                wd_ld[u] = S.add("gpsimd", lambda g, d2t=d2t, grp=grp, u=u: g.dma_start(
                    xn_stage[:, 1024 * (u % 2):1024 * (u % 2 + 1)],
                    wd_hbm[128 * d2t:128 * (d2t + 1), 1024 * grp:1024 * (grp + 1)]),
                    waits=w, inc=q_wdi[u % 2], amt=16)

            def wd_unit(u):
                d2t, grp = u // 3, u % 3
                sl = u % 2
                for j in range(8):
                    w = []
                    if j == 0:
                        w = [(q_wdi[sl], wd_ld[u]), (s_evT, evT_base + NT)]
                        if u >= 2:
                            w.append((s_wdev, wdev0 + u - 1))
                    S.add("tensor", lambda tt, sl=sl, j=j: tt.transpose(
                        ps_tp[sl][:, 128 * j:128 * (j + 1)],
                        xn_stage[:, 1024 * sl + 128 * j:1024 * sl + 128 * (j + 1)],
                        ident[:, :]),
                        waits=w, inc=(s_wT if j == 7 else None))
                sl4 = u % 4
                evw = [(s_wT, wdtp0 + u + 1)]
                if u >= 4:
                    evw.append((q_wdw[sl4], 16 * ((u - 4) // 4 + 1)))
                ev = S.add("vector", lambda v, sl4=sl4: v.tensor_copy(
                    wks_sb[:, 1024 * sl4:1024 * (sl4 + 1)], ps_tp[sl][:, :]),
                    waits=evw, inc=s_wdev)
                dstw = bass.AP(wdT_dram.tensor, (8 * grp) * 128 * 1024 + 128 * d2t,
                               [[1024, 128], [128 * 1024, 8], [1, 128]])
                S.add("sync", lambda s, dstw=dstw, sl4=sl4: s.dma_start(
                    dstw, wks_sb[:, 1024 * sl4:1024 * (sl4 + 1)]),
                    waits=[(s_wdev, ev)], inc=q_wdw[sl4], amt=16)

            def wd_tick(frac):
                want = min(24, int(frac * 24))
                if want > 0 and wd_emitted[0] == 0:
                    wd_stage_dma(0)
                    wd_stage_dma(1)
                while wd_emitted[0] < want:
                    u = wd_emitted[0]
                    if u + 2 < 24:
                        wd_stage_dma(u + 2)
                    wd_unit(u)
                    wd_emitted[0] += 1
        else:
            def wd_tick(frac):
                pass

        for gi, (kind, i1, i2) in enumerate(proj_groups):
            if kind == "k" and i1 == 0 and i2 == 0:
                dstv = bass.AP(w_pg[0], 0, [[4096, 128], [1024, 4], [1, 1024]])
                srcv = bass.AP(mv_hbm.tensor, 0, [[D, 128], [128 * D, 4], [1, D]])
                wv_lds = [S.add(
                    "gpsimd", lambda g, dstv=dstv, srcv=srcv: g.dma_start(dstv, srcv),
                    waits=[(s_proj, proj_base + 16)], inc=q_wv, amt=16),
                    wv1_early]

            slot = gi % 4
            for dt in range(8):
                if kind == "q":
                    lhsT = w_pg[0][:, 512 * dt + 128 * i1:512 * dt + 128 * (i1 + 1)]
                    rhs = xnT[:, 2048 * dt + 512 * i2:2048 * dt + 512 * (i2 + 1)]
                    wwait = (q_wq, wq_ld)
                elif kind == "k":
                    lhsT = w_pg[1][:, 512 * dt + 128 * i1:512 * dt + 128 * (i1 + 1)]
                    rhs = xnT[:, 2048 * dt + 512 * i2:2048 * dt + 512 * (i2 + 1)]
                    wwait = (q_wk, wk_ld)
                else:
                    lhsT = xnT[:, 2048 * dt + 128 * i1:2048 * dt + 128 * (i1 + 1)]
                    wslot = w_pg[0] if dt < 4 else kTs
                    col = 1024 * (dt % 4) + 512 * i2
                    rhs = wslot[:, col:col + 512]
                    wwait = (q_wv, wv_lds[0])
                waits = []
                if kind == "v" and dt == 4:
                    waits = [(q_wv1, wv_lds[1])]
                if dt == 0:
                    if kind == "v":
                        evt_need = evT_base + i1 + 1
                    else:
                        evt_need = evT_base + 4 * i2 + 4
                    waits = [wwait, (s_evT, evt_need)]
                    if gi >= 4:
                        if gi % 2 == 0:
                            waits.append((s_evP, evP_base + gi // 2 - 1))
                        else:
                            waits.append((s_evPA, evPA_base + (gi - 3) // 2))
                S.add("tensor", lambda tt, lhsT=lhsT, rhs=rhs, slot=slot, dt=dt: tt.matmul(
                    psP[slot][:, :], lhsT, rhs, start=(dt == 0), stop=(dt == 7)),
                    waits=waits, inc=(s_proj if dt == 7 else None))
            if kind == "q":
                dst = qT[:, 2048 * i1 + 512 * i2: 2048 * i1 + 512 * (i2 + 1)]
            elif kind == "k":
                dst = kT[:, 2048 * i1 + 512 * i2: 2048 * i1 + 512 * (i2 + 1)]
            else:
                dst = v_sb[:, D * i1 + 512 * i2: D * i1 + 512 * (i2 + 1)]
            if gi % 2 == 0:
                S.add("vector", lambda v, dst=dst, slot=slot: v.tensor_copy(dst, psP[slot][:, :]),
                      waits=[(s_proj, proj_base + gi + 1)], inc=s_evP)
            else:
                S.add("scalar", lambda sc, dst=dst, slot=slot: sc.copy(dst, psP[slot][:, :]),
                      waits=[(s_proj, proj_base + gi + 1)], inc=s_evPA)
            if kind == "k" and i2 == 3 and i1 >= 2:
                kw = [(s_evP, evP_base + 10 + 2 * i1), (s_evPA, evPA_base + 10 + 2 * i1)]
                S.add("sync", lambda s, i1=i1: s.dma_start(
                    kTs[0:64, 2048 * i1:2048 * (i1 + 1)],
                    kT[64:128, 2048 * i1:2048 * (i1 + 1)]),
                    waits=kw, inc=q_ksw, amt=16)
                S.add("sync", lambda s, i1=i1: s.dma_start(
                    kTs[64:128, 2048 * i1:2048 * (i1 + 1)],
                    kT[0:64, 2048 * i1:2048 * (i1 + 1)]),
                    inc=q_ksw, amt=16)
            if gi >= 32:
                wd_tick((gi - 31) / 32.0)
        wd_tick(1.0)
        for i1 in range(2):
            S.add("sync", lambda s, i1=i1: s.dma_start(
                kTs[0:64, 2048 * i1:2048 * (i1 + 1)],
                kT[64:128, 2048 * i1:2048 * (i1 + 1)]),
                waits=[(s_proj, proj_base + 64)] if i1 == 0 else [], inc=q_ksw, amt=16)
            S.add("sync", lambda s, i1=i1: s.dma_start(
                kTs[64:128, 2048 * i1:2048 * (i1 + 1)],
                kT[0:64, 2048 * i1:2048 * (i1 + 1)]),
                inc=q_ksw, amt=16)
        if debug_dump and b == 0:
            S.add("gpsimd", lambda g: g.dma_start(out_hbm[1, 0:128, :], x_sb[:, 0:D]),
                  waits=[(s_evP, S.cnt[s_evP]), (s_evPA, S.cnt[s_evPA])], inc=q_st[0], amt=16)
            S.add("gpsimd", lambda g: g.dma_start(out_hbm[1, 128:256, :], xnT[:, 0:D]),
                  inc=q_st[0], amt=16)
            S.add("gpsimd", lambda g: g.dma_start(out_hbm[1, 256:384, :], qT[:, 0:D]),
                  inc=q_st[0], amt=16)
            S.add("gpsimd", lambda g: g.dma_start(out_hbm[1, 384:512, :], kT[:, 0:D]),
                  inc=q_st[0], amt=16)
            S.add("gpsimd", lambda g: g.dma_start(out_hbm[1, 512:640, :], kTs[:, 0:D]),
                  inc=q_st[0], amt=16)
            S.add("gpsimd", lambda g: g.dma_start(out_hbm[1, 640:768, :], v_sb[:, 0:D]),
                  inc=q_st[0], amt=16)
            S.add("sync", None, waits=[(q_st[0], S.cnt[q_st[0]])])
        S.flush(f"a1_{b}")
        a1.close()

        if b == 0:
            for e in ("scalar", "vector", "gpsimd"):
                S.add(e, None, waits=[(q_wdw[i], S.cnt[q_wdw[i]]) for i in range(4)])
        a2 = ExitStack()
        att = sb(f"att_{b}", [128, 8 * 2048], BF16, a2)
        u_st = sb(f"u_{b}", [128, 2 * 2048], BF16, a2)
        Sp = [pt(f"S{i}_{b}", [128, 1024], F32, a2) for i in range(2)]
        Op = [pt(f"O{i}_{b}", [128, 1024], F32, a2) for i in range(2)]

        mm_base = S.cnt[s_mm]
        exp_base = S.cnt[s_exp]
        scale_base = S.cnt[s_scale]
        av_base = S.cnt[s_av]
        apply_base = S.cnt[s_apply]
        xadd_base = S.cnt[s_xadd]
        a2_add_bases[b] = (xadd_base, S.cnt[s_xaddd])

        def emit_av_step(g, t):
            h = t // 2
            sl6 = (16 * g + t) % 8
            for m in range(8):
                for dlt in range(2):
                    lhsT = att[:, 2048 * sl6 + 256 * m + 128 * dlt:
                               2048 * sl6 + 256 * m + 128 * (dlt + 1)]
                    rhs = v_sb[:, D * (2 * m + t % 2) + 128 * h:
                               D * (2 * m + t % 2) + 128 * (h + 1)]
                    waits = []
                    if m == 0 and dlt == 0:
                        waits = [(s_scale, scale_base + 16 * g + t + 1)]
                        if t == 0:
                            waits.append((s_apply, apply_base + 2 * g))
                    inc = s_av if (m == 7 and dlt == 1) else None
                    S.add("tensor", lambda tt, lhsT=lhsT, rhs=rhs, dlt=dlt, m=m, t=t: tt.matmul(
                        Op[dlt][:, 128 * m:128 * (m + 1)], lhsT, rhs,
                        start=(t == 0 and m % 4 == 0), stop=(t == 15 and m % 4 == 3)),
                        waits=waits, inc=inc)

        pending_rstd = None
        pending_dve_tail = None

        def emit_rstd(pr):
            l2 = S.add("scalar", lambda sc: sc.activation(rstd2[:, :], vfix2[:, :], AF.Ln),
                       waits=[(c_dve, pr[0])], inc=c_act)
            e2 = S.add("scalar", lambda sc: sc.activation(rstd2[:, :], rstd2[:, :],
                                                          AF.Exp, scale=-0.5),
                       waits=[(c_act, l2)], inc=c_act)
            assert e2 == pr[1], (e2, pr)

        for g in range(8):
            for k in range(32):
                if g > 0 and k in (0, 2, 4, 6, 8):
                    emit_av_step(g - 1, 11 + k // 2)
                K = 32 * g + k
                t, c = k // 2, k % 2
                hh = t // 2
                for mi in range(4):
                    m = 2 * mi + c
                    if (hh % 2) == (m % 2):
                        ksrc, kbase = kT, 64 * (hh % 2)
                    else:
                        ksrc, kbase = kTs, 64 * (1 - hh % 2)
                    lhsT = ksrc[kbase:kbase + 64,
                                2048 * (hh // 2) + 256 * g + 128 * (t % 2):
                                2048 * (hh // 2) + 256 * g + 128 * (t % 2) + 128]
                    rhs = qT[64 * (m % 2):64 * (m % 2) + 64,
                             2048 * (m // 2) + 256 * g:2048 * (m // 2) + 256 * g + 256]
                    waits = []
                    if mi == 0:
                        waits = [(s_exp, exp_base + K - 1)]
                        if g == 0 and k == 1:
                            waits.append((q_ksw, S.cnt[q_ksw]))
                    S.add("tensor", lambda tt, lhsT=lhsT, rhs=rhs, mi=mi, K=K: tt.matmul(
                        Sp[K % 2][:, 256 * mi:256 * (mi + 1)], lhsT, rhs,
                        start=True, stop=True),
                        waits=waits, inc=(s_mm if mi == 3 else None))
                if k >= 15 and k % 2 == 1:
                    emit_av_step(g, (k - 15) // 2)
            emit_av_step(g, 9)
            emit_av_step(g, 10)
            if g == 7:
                for tav in (11, 12, 13, 14, 15):
                    emit_av_step(g, tav)

            def emit_tile_chain(t, g=g):
                sl6 = (16 * g + t) % 8
                w1 = S.add("vector", lambda v, sl6=sl6: v.tensor_tensor(
                    ssum[:, sl6:sl6 + 1], accx[:, 2 * sl6:2 * sl6 + 1],
                    accx[:, 2 * sl6 + 1:2 * sl6 + 2], AL.add),
                    waits=[(s_exp, exp_base + 32 * g + 2 * t + 2)], inc=c_dve)
                w2 = S.add("vector", lambda v, sl6=sl6: v.reciprocal(
                    rsx[:, sl6:sl6 + 1], ssum[:, sl6:sl6 + 1]),
                    waits=[(c_dve, w1)], inc=c_dve)
                S.add("vector", lambda v, sl6=sl6: v.tensor_scalar(
                    att[:, 2048 * sl6:2048 * (sl6 + 1)], att[:, 2048 * sl6:2048 * (sl6 + 1)],
                    rsx[:, sl6:sl6 + 1], None, AL.mult),
                    waits=[(c_dve, w2)], inc=s_scale)

            if g >= 6:
                hoist_b1_stats(b, 0, g - 6)
            for t in range(3):
                emit_tile_chain(t)
            if pending_dve_tail is not None:
                pending_rstd = pending_dve_tail()
                pending_dve_tail = None
            for t in range(3, 16):
                emit_tile_chain(t)

            def make_tail(g=g):
                def tail():
                    avdone = av_base + 16 * g + 16
                    for dlt in range(2):
                        S.add("vector", lambda v, dlt=dlt: v.bn_stats(
                            bn2[:, 12 * dlt:12 * dlt + 6], Op[dlt][:, 0:512]),
                            waits=[(s_av, avdone)] if dlt == 0 else [])
                        last = S.add("vector", lambda v, dlt=dlt: v.bn_stats(
                            bn2[:, 12 * dlt + 6:12 * dlt + 12], Op[dlt][:, 512:1024]),
                            inc=c_dve)
                    for dlt in range(2):
                        last = S.add("vector", lambda v, dlt=dlt: v.bn_aggr(
                            mv2[:, 2 * dlt:2 * dlt + 2], bn2[:, 12 * dlt:12 * dlt + 12]),
                            waits=[(c_dve, last)] if dlt == 0 else [], inc=c_dve)
                    var2 = bass.AP(mv2, 1, [[4, 128], [2, 2]])
                    mean2 = bass.AP(mv2, 0, [[4, 128], [2, 2]])
                    v2d = S.add("vector", lambda v: v.tensor_scalar(
                        vfix2[:, :], var2, EPS, None, AL.add),
                        waits=[(c_dve, last)], inc=c_dve)
                    negm_done = S.add("vector", lambda v: v.tensor_scalar(
                        nmr2[:, :], mean2, -1.0, None, AL.mult),
                        waits=[(c_dve, v2d)], inc=c_dve)
                    e2_pred = S.cnt[c_act] + 2
                    for dlt in range(2):
                        S.add("vector", lambda v, dlt=dlt, g=g: v.tensor_scalar(
                            u_st[:, 2048 * (g % 2) + 1024 * dlt:2048 * (g % 2) + 1024 * (dlt + 1)],
                            Op[dlt][:, :], nmr2[:, dlt:dlt + 1], rstd2[:, dlt:dlt + 1],
                            AL.add, AL.mult),
                            waits=[(c_act, e2_pred), (c_dve, negm_done),
                                   (s_xadd, xadd_base + 2 * (g - 2) + dlt + 1)],
                            inc=s_apply)
                    return (v2d, e2_pred)
                return tail

            pending_dve_tail = make_tail()

            for k in range(32):
                if k == 10 and pending_rstd is not None:
                    emit_rstd(pending_rstd)
                    pending_rstd = None
                K = 32 * g + k
                t, c = k // 2, k % 2
                sl6 = (16 * g + t) % 8
                att_out = bass.AP(att, 2048 * sl6 + 256 * c,
                                  [[8 * 2048, 128], [512, 4], [1, 256]])
                S.add("scalar", lambda sc, K=K, att_out=att_out, sl6=sl6, c=c: sc.activation(
                    att_out,
                    Sp[K % 2][:, :], AF.Exp, scale=0.125,
                    accum_out=accx[:, 2 * sl6 + c:2 * sl6 + c + 1]),
                    waits=[(s_mm, mm_base + K + 1),
                           (s_av, av_base + 16 * g + t - 7)],
                    inc=s_exp)

            for dlt in range(2):
                if g == 7:
                    continue
                ttile = 2 * g + dlt
                S.add("gpsimd", lambda p, ttile=ttile, dlt=dlt, g=g: p.tensor_tensor(
                    x_sb[:, D * ttile:D * (ttile + 1)], x_sb[:, D * ttile:D * (ttile + 1)],
                    u_st[:, 2048 * (g % 2) + 1024 * dlt:2048 * (g % 2) + 1024 * (dlt + 1)],
                    AL.add),
                    waits=[(s_apply, apply_base + 2 * g + dlt + 1)], inc=s_xadd)
        pending_rstd = pending_dve_tail()
        pending_dve_tail = None
        emit_rstd(pending_rstd)
        pending_rstd = None
        for dlt in range(2):
            ttile = 14 + dlt
            S.add("vector", lambda v, ttile=ttile, dlt=dlt: v.tensor_tensor(
                x_sb[:, D * ttile:D * (ttile + 1)], x_sb[:, D * ttile:D * (ttile + 1)],
                u_st[:, 2048 + 1024 * dlt:2048 + 1024 * (dlt + 1)],
                AL.add),
                waits=[(s_apply, apply_base + 14 + dlt + 1)], inc=s_xaddd)
        if debug_dump and b == 0:
            S.add("gpsimd", lambda g: g.dma_start(out_hbm[1, 768:896, :], x_sb[:, 0:D]),
                  waits=[(s_xadd, S.cnt[s_xadd])], inc=q_st[1], amt=16)
            S.add("gpsimd", lambda g: g.dma_start(out_hbm[1, 896:1024, :], u_st[:, 0:D]),
                  inc=q_st[1], amt=16)
            S.add("gpsimd", lambda g: g.dma_start(out_hbm[1, 1024:1152, :], att[:, 0:D]),
                  inc=q_st[1], amt=16)
            S.add("gpsimd", lambda g: g.dma_start(out_hbm[1, 1152:1280, :], att[:, 2048:2048 + D]),
                  inc=q_st[1], amt=16)
            S.add("sync", None, waits=[(q_st[1], S.cnt[q_st[1]])])
        S.flush(f"a2_{b}")
        a2.close()
        bs.close()

        fb = ExitStack()
        wdT = sb(f"wdT_{b}", [128, NHT * 1024], BF16, fb)

        for H in range(2):
            fh = ExitStack()
            xn2_bf = xn_stage
            hT = sb(f"hT_{b}_{H}", [128, NHT * 1024], BF16, fh)
            wds = ExitStack()

            inline_tp = (b == 0 and H == 0)
            p1 = ExitStack()
            x2T = sb(f"x2T_{b}_{H}", [128, 8 * 1024], BF16, p1)
            stripeW = D if inline_tp else 2 * D
            wgT_s = [sb(f"wgT{i}_{b}_{H}", [128, stripeW], BF16, p1) for i in range(2)]
            wuT_s = [sb(f"wuT{i}_{b}_{H}", [128, stripeW], BF16, p1) for i in range(2)]
            if inline_tp:
                wg_st = [sb(f"wgst{i}_{b}", [128, D], BF16, p1) for i in range(2)]
                wu_st = [sb(f"wust{i}_{b}", [128, D], BF16, p1) for i in range(2)]
                psB = [pt(f"psB{i}_{b}", [128, 1024], BF16, p1) for i in range(4)]
                ps_t2 = psB[:2]
                psTw = psB[:2]
                psTu = psB[2:]
            else:
                ps_t2 = [pt(f"pst2{i}_{b}_{H}", [128, 1024], BF16, p1) for i in range(2)]
            NPS = 2 if inline_tp else 3
            psG = [pt(f"psG{i}_{b}_{H}", [128, 512], F32, p1) for i in range(NPS)]
            psU = [pt(f"psU{i}_{b}_{H}", [128, 512], F32, p1) for i in range(NPS)]

            toks = [8 * H + j for j in range(8)]
            stats_last = {}

            def emit_bstats(hf):
                last = None
                for j in range(4 * hf, 4 * hf + 4):
                    t = toks[j]
                    S.add("vector", lambda v, j=j, t=t: v.bn_stats(
                        bn1[:, 12 * j:12 * j + 6], x_sb[:, D * t:D * t + 512]))
                    last = S.add("vector", lambda v, j=j, t=t: v.bn_stats(
                        bn1[:, 12 * j + 6:12 * j + 12], x_sb[:, D * t + 512:D * t + 1024]),
                        inc=c_dve)
                for j in range(4 * hf, 4 * hf + 4):
                    last = S.add("vector", lambda v, j=j: v.bn_aggr(
                        mvs[:, 2 * j:2 * j + 2], bn1[:, 12 * j:12 * j + 12]),
                        waits=[(c_dve, last)] if j == 4 * hf else [], inc=c_dve)
                stats_last[hf] = last

            nm_done = {}

            def emit_brstd(hf):
                var_v = bass.AP(mvs, 8 * hf + 1, [[2 * NT, 128], [2, 4]])
                mean_v = bass.AP(mvs, 8 * hf, [[2 * NT, 128], [2, 4]])
                c4 = slice(4 * hf, 4 * hf + 4)
                vd = S.add("vector", lambda v, c4=c4, var_v=var_v: v.tensor_scalar(
                    vfix1[:, c4], var_v, EPS, None, AL.add),
                    waits=[(c_dve, stats_last[hf])], inc=c_dve)
                ld2 = S.add("scalar", lambda sc, c4=c4: sc.activation(
                    rstd1[:, c4], vfix1[:, c4], AF.Ln),
                    waits=[(c_dve, vd)], inc=c_act)
                ed2 = S.add("scalar", lambda sc, c4=c4: sc.activation(
                    rstd1[:, c4], rstd1[:, c4], AF.Exp, scale=-0.5),
                    waits=[(c_act, ld2)], inc=c_act)
                nm1 = S.add("vector", lambda v, c4=c4, mean_v=mean_v: v.tensor_tensor(
                    nmr1[:, c4], mean_v, rstd1[:, c4], AL.mult),
                    waits=[(c_act, ed2)], inc=c_dve)
                nm_done[hf] = S.add("vector", lambda v, c4=c4: v.tensor_scalar(
                    nmr1[:, c4], nmr1[:, c4], -1.0, None, AL.mult),
                    waits=[(c_dve, nm1)], inc=c_dve)

            tpb = S.cnt[s_tp]
            xnb = S.cnt[s_xn]
            ev_mark = {-2: None, -1: None}

            def emit_bapply(j):
                t = toks[j]
                S.add("scalar", lambda sc, j=j, t=t: sc.activation(
                    xn2_bf[:, D * (j % 2):D * (j % 2 + 1)], x_sb[:, D * t:D * (t + 1)],
                    AF.Identity, bias=nmr1[:, j:j + 1], scale=rstd1[:, j:j + 1]),
                    waits=[(c_dve, nm_done[j // 4]), (s_tp, tpb + j - 1)], inc=s_xn)
                for dt in range(8):
                    w = []
                    if dt == 0:
                        w = [(s_xn, xnb + j + 1)]
                        if ev_mark[j - 2] is not None:
                            w.append(ev_mark[j - 2])
                    S.add("tensor", lambda tt, j=j, dt=dt: tt.transpose(
                        ps_t2[j % 2][:, 128 * dt:128 * (dt + 1)],
                        xn2_bf[:, D * (j % 2) + 128 * dt:D * (j % 2) + 128 * (dt + 1)],
                        ident[:, :]),
                        waits=w, inc=(s_tp if dt == 7 else None))

            def emit_bevac(j):
                dest = bass.AP(x2T, 128 * j, [[8 * 1024, 128], [1024, 8], [1, 128]])
                if j % 2 == 0:
                    c = S.add("scalar", lambda sc, dest=dest, j=j: sc.copy(
                        dest, ps_t2[j % 2][:, :]),
                        waits=[(s_tp, tpb + j + 1)], inc=s_evT)
                    ev_mark[j] = (s_evT, c)
                else:
                    c = S.add("vector", lambda v, dest=dest, j=j: v.tensor_copy(
                        dest, ps_t2[j % 2][:, :]),
                        waits=[(s_tp, tpb + j + 1)], inc=s_evD)
                    ev_mark[j] = (s_evD, c)

            stats_last[0] = b1_stats_last[(b, H, 0)]
            stats_last[1] = b1_stats_last[(b, H, 1)]
            emit_brstd(0)
            for j in range(4):
                emit_bapply(j)
                if j > 0:
                    emit_bevac(j - 1)
            emit_brstd(1)
            for j in range(4, 8):
                emit_bapply(j)
                emit_bevac(j - 1)
            emit_bevac(7)
            x2T_c0 = [ev_mark[2], ev_mark[3]]
            x2T_w = [ev_mark[6], ev_mark[7]]

            gup_base = S.cnt[s_gup]
            silu_base = S.cnt[s_silu]
            hT_base = S.cnt[s_hT]
            ldT_g = {}
            ldT_u = {}

            if inline_tp:
                wT_base = S.cnt[s_wT]
                wgev_base = S.cnt[s_wgev]
                wuev_base = S.cnt[s_wuev]
                wgw_base = [S.cnt[q_wgw[0]], S.cnt[q_wgw[1]]]
                wuw_base = [S.cnt[q_wuw[0]], S.cnt[q_wuw[1]]]

                def emit_wdma(ht):
                    sl = ht % 2
                    ldT_g[ht] = S.add("gpsimd", lambda g, ht=ht, sl=sl: g.dma_start(
                        wg_st[sl][:, :], wg_hbm[128 * ht:128 * (ht + 1), :]),
                        waits=[(s_wT, wT_base + 2 * (ht - 2) + 1)], inc=q_wgi[sl], amt=16)
                    ldT_u[ht] = S.add("gpsimd", lambda g, ht=ht, sl=sl: g.dma_start(
                        wu_st[sl][:, :], wu_hbm[128 * ht:128 * (ht + 1), :]),
                        waits=[(s_wT, wT_base + 2 * (ht - 2) + 2)], inc=q_wui[sl], amt=16)

                def emit_wtp(ht):
                    sl = ht % 2
                    for dt in range(8):
                        S.add("tensor", lambda tt, sl=sl, dt=dt: tt.transpose(
                            psTw[sl][:, 128 * dt:128 * (dt + 1)],
                            wg_st[sl][:, 128 * dt:128 * (dt + 1)], ident[:, :]),
                            waits=([(q_wgi[sl], ldT_g[ht]), (s_wgev, wgev_base + (ht - 2) + 1)]
                                   + x2T_w if dt == 0 else []),
                            inc=(s_wT if dt == 7 else None))
                    for dt in range(8):
                        S.add("tensor", lambda tt, sl=sl, dt=dt: tt.transpose(
                            psTu[sl][:, 128 * dt:128 * (dt + 1)],
                            wu_st[sl][:, 128 * dt:128 * (dt + 1)], ident[:, :]),
                            waits=([(q_wui[sl], ldT_u[ht]), (s_wuev, wuev_base + (ht - 2) + 1)]
                                   if dt == 0 else []),
                            inc=(s_wT if dt == 7 else None))

                def emit_wev(ht):
                    sl = ht % 2
                    evg = S.add("scalar", lambda sc, sl=sl: sc.copy(wgT_s[sl][:, :], psTw[sl][:, :]),
                                waits=[(s_wT, wT_base + 2 * ht + 1),
                                       (q_wgw[sl], wgw_base[sl] + 16 * ((ht - 2) // 2 + 1)
                                        if ht >= 2 else 0)],
                                inc=s_wgev)
                    S.add("sync", lambda s, ht=ht, sl=sl: s.dma_start(
                        wgT_dram[ht, :, :], wgT_s[sl][:, :]),
                        waits=[(s_wgev, evg)], inc=q_wgw[sl], amt=16)
                    evu = S.add("vector", lambda v, sl=sl: v.tensor_copy(wuT_s[sl][:, :], psTu[sl][:, :]),
                                waits=[(s_wT, wT_base + 2 * ht + 2),
                                       (q_wuw[sl], wuw_base[sl] + 16 * ((ht - 2) // 2 + 1)
                                        if ht >= 2 else 0)],
                                inc=s_wuev)
                    S.add("sync", lambda s, ht=ht, sl=sl: s.dma_start(
                        wuT_dram[ht, :, :], wuT_s[sl][:, :]),
                        waits=[(s_wuev, evu)], inc=q_wuw[sl], amt=16)

                emit_wdma(0)
                emit_wdma(1)
                emit_wtp(0)
                emit_wev(0)
                emit_wtp(1)
                emit_wev(1)
            else:
                def emit_pair_load(p):
                    sl = p % 2
                    wgr = [(s_gup, max(gup_base + 8 * p - 9, 0))]
                    wur = [(s_gup, max(gup_base + 8 * p - 8, 0))]
                    if p < 2 and b == 0:
                        wgr += [(q_wgw[0], S.cnt[q_wgw[0]]), (q_wgw[1], S.cnt[q_wgw[1]])]
                        wur += [(q_wuw[0], S.cnt[q_wuw[0]]), (q_wuw[1], S.cnt[q_wuw[1]])]
                    srcg = bass.AP(wgT_dram.tensor, 2 * p * 128 * 1024,
                                   [[1024, 128], [128 * 1024, 2], [1, 1024]])
                    srcu = bass.AP(wuT_dram.tensor, 2 * p * 128 * 1024,
                                   [[1024, 128], [128 * 1024, 2], [1, 1024]])
                    ldT_g[p] = S.add("sync", lambda s, sl=sl, srcg=srcg: s.dma_start(
                        wgT_s[sl][:, :], srcg),
                        waits=wgr, inc=q_wgt[sl], amt=16)
                    ldT_u[p] = S.add("sync", lambda s, sl=sl, srcu=srcu: s.dma_start(
                        wuT_s[sl][:, :], srcu),
                        waits=wur, inc=q_wut[sl], amt=16)

                emit_pair_load(0)
                emit_pair_load(1)
            if H == 0:
                wdld = bass.AP(wdT_dram.tensor, 0,
                               [[1024, 128], [128 * 1024, NHT], [1, 1024]])
                wdst = bass.AP(wdT, 0, [[NHT * 1024, 128], [1024, NHT], [1, 1024]])
                S.add("sync", lambda s, wdst=wdst, wdld=wdld: s.dma_start(wdst, wdld),
                      waits=[(q_wdw[i], S.cnt[q_wdw[i]]) for i in range(4)],
                      inc=q_wdT, amt=16)
            for ht in range(NHT):
                sl = ht % 2
                if inline_tp:
                    wsl, wcol = sl, 0
                    if ht + 2 < NHT:
                        emit_wdma(ht + 2)
                else:
                    wsl, wcol = (ht // 2) % 2, D * (ht % 2)
                    if ht % 2 == 0 and ht // 2 + 2 < NHT // 2:
                        emit_pair_load(ht // 2 + 2)
                for c2 in range(2):
                    ci = 2 * ht + c2
                    gsl = ci % NPS
                    for dt in range(8):
                        waits = []
                        if dt == 0:
                            waits = ([(s_wgev, wgev_base + ht + 1)] if inline_tp
                                     else [(q_wgt[wsl], ldT_g[ht // 2])]) + [
                                     (s_silu, silu_base + ci - NPS + 1)] + (
                                     x2T_c0 if c2 == 0 else x2T_w)
                        S.add("tensor", lambda tt, wsl=wsl, wcol=wcol, dt=dt, gsl=gsl, c2=c2: tt.matmul(
                            psG[gsl][:, :], wgT_s[wsl][:, wcol + 128 * dt:wcol + 128 * (dt + 1)],
                            x2T[:, 1024 * dt + 512 * c2:1024 * dt + 512 * (c2 + 1)],
                            start=(dt == 0), stop=(dt == 7)),
                            waits=waits, inc=(s_gup if dt == 7 else None))
                    for dt in range(8):
                        waits = []
                        if dt == 0:
                            waits = ([(s_wuev, wuev_base + ht + 1)] if inline_tp
                                     else [(q_wut[wsl], ldT_u[ht // 2])]) + [
                                     (s_hT, hT_base + ci - NPS + 1)]
                        S.add("tensor", lambda tt, wsl=wsl, wcol=wcol, dt=dt, gsl=gsl, c2=c2: tt.matmul(
                            psU[gsl][:, :], wuT_s[wsl][:, wcol + 128 * dt:wcol + 128 * (dt + 1)],
                            x2T[:, 1024 * dt + 512 * c2:1024 * dt + 512 * (c2 + 1)],
                            start=(dt == 0), stop=(dt == 7)),
                            waits=waits, inc=(s_gup if dt == 7 else None))
                for c2 in range(2):
                    ci = 2 * ht + c2
                    gsl = ci % NPS
                    dstH = hT[:, 1024 * ht + 512 * c2:1024 * ht + 512 * (c2 + 1)]
                    S.add("scalar", lambda sc, gsl=gsl, dstH=dstH: sc.activation(
                        dstH, psG[gsl][:, :], AF.Silu, scale=1.0 / 127.0),
                        waits=[(s_gup, gup_base + 4 * ht + 2 * c2 + 1)], inc=s_silu)
                    S.add("vector", lambda v, gsl=gsl, dstH=dstH: v.tensor_tensor(
                        dstH, dstH, psU[gsl][:, :], AL.mult),
                        waits=[(s_silu, silu_base + ci + 1),
                               (s_gup, gup_base + 4 * ht + 2 * c2 + 2)], inc=s_hT)
                if inline_tp and ht + 2 < NHT:
                    emit_wtp(ht + 2)
                    emit_wev(ht + 2)
            p1.close()

            p2 = ExitStack()
            u2 = sb(f"u2_{b}_{H}", [128, 2 * 2048], F32, p2)
            psD = [[pt(f"psD{i}{tau}_{b}_{H}", [128, 1024], F32, p2) for tau in range(2)]
                   for i in range(2)]
            down_base = S.cnt[s_down]
            ap3_base = S.cnt[s_ap3]
            xadd3_base = S.cnt[s_xadd3]
            hT_total = S.cnt[s_hT]
            wdT_ready = (q_wdT, 16 * (b + 1))
            add_d = {}
            for Sc in range(4):
                slot = Sc % 2
                for ht in range(NHT):
                    for tau in range(2):
                        for d2h in range(2):
                            waits = []
                            if ht == 0 and tau == 0 and d2h == 0:
                                waits = [(s_hT, hT_total), wdT_ready,
                                         (s_ap3, ap3_base + 2 * (Sc - 2) + 2)]
                            S.add("tensor", lambda tt, slot=slot, tau=tau, d2h=d2h, ht=ht, Sc=Sc: tt.matmul(
                                psD[slot][tau][:, 512 * d2h:512 * (d2h + 1)],
                                hT[:, 1024 * ht + 256 * Sc + 128 * tau:
                                   1024 * ht + 256 * Sc + 128 * (tau + 1)],
                                wdT[:, 1024 * ht + 512 * d2h:1024 * ht + 512 * (d2h + 1)],
                                start=(ht == 0), stop=(ht == NHT - 1)),
                                waits=waits,
                                inc=(s_down if (ht == NHT - 1 and tau == 1 and d2h == 1) else None))
                dn = S.cnt[s_down]
                for tau in range(2):
                    S.add("vector", lambda v, slot=slot, tau=tau: v.bn_stats(
                        bn2[:, 12 * tau:12 * tau + 6], psD[slot][tau][:, 0:512]),
                        waits=[(s_down, dn),
                               (s_ap3, ap3_base + 2 * (Sc - 1) + 2)] if tau == 0 else [])
                    last3 = S.add("vector", lambda v, slot=slot, tau=tau: v.bn_stats(
                        bn2[:, 12 * tau + 6:12 * tau + 12], psD[slot][tau][:, 512:1024]),
                        inc=c_dve)
                for tau in range(2):
                    last3 = S.add("vector", lambda v, tau=tau: v.bn_aggr(
                        mv2[:, 2 * tau:2 * tau + 2], bn2[:, 12 * tau:12 * tau + 12]),
                        waits=[(c_dve, last3)] if tau == 0 else [], inc=c_dve)
                var3 = bass.AP(mv2, 1, [[4, 128], [2, 2]])
                mean3 = bass.AP(mv2, 0, [[4, 128], [2, 2]])
                v3 = S.add("vector", lambda v: v.tensor_scalar(
                    vfix2[:, :], var3, EPS3, None, AL.add),
                    waits=[(c_dve, last3)], inc=c_dve)
                l3 = S.add("scalar", lambda sc: sc.activation(rstd2[:, :], vfix2[:, :], AF.Ln),
                           waits=[(c_dve, v3)], inc=c_act)
                e3 = S.add("scalar", lambda sc: sc.activation(rstd2[:, :], rstd2[:, :], AF.Exp, scale=-0.5),
                           waits=[(c_act, l3)], inc=c_act)
                n31 = S.add("vector", lambda v: v.tensor_tensor(nmr2[:, :], mean3, rstd2[:, :], AL.mult),
                            waits=[(c_act, e3)], inc=c_dve)
                n32 = S.add("vector", lambda v: v.tensor_scalar(nmr2[:, :], nmr2[:, :], -1.0, None, AL.mult),
                            waits=[(c_dve, n31)], inc=c_dve)
                if H == 0 and Sc in (1, 2):
                    hoist_b1_stats(b, 1, Sc - 1)
                if b == 0 and H == 1 and Sc in (1, 2):
                    hoist_a1_stats(1, Sc - 1)
                if b == 0 and H == 1 and Sc == 2:
                    hoist_a1_stats(1, 2)
                for tau in range(2):
                    S.add("scalar", lambda sc, slot=slot, tau=tau, Sc=Sc: sc.activation(
                        u2[:, 2048 * slot + 1024 * tau:2048 * slot + 1024 * (tau + 1)],
                        psD[slot][tau][:, :], AF.Identity,
                        bias=nmr2[:, tau:tau + 1], scale=rstd2[:, tau:tau + 1]),
                        waits=[(c_dve, n32),
                               (s_xadd3, xadd3_base + 2 * (Sc - 2) + tau + 1)],
                        inc=s_ap3)
                for tau in range(2):
                    tt_idx = 8 * H + 2 * Sc + tau
                    if Sc == 3:
                        add_d[tau] = S.add("vector", lambda v, slot=slot, tau=tau, tt_idx=tt_idx: v.tensor_tensor(
                            x_sb[:, D * tt_idx:D * (tt_idx + 1)],
                            x_sb[:, D * tt_idx:D * (tt_idx + 1)],
                            u2[:, 2048 * slot + 1024 * tau:2048 * slot + 1024 * (tau + 1)],
                            AL.add),
                            waits=[(s_ap3, ap3_base + 2 * Sc + tau + 1)], inc=s_xadd3d)
                    else:
                        S.add("gpsimd", lambda p, slot=slot, tau=tau, tt_idx=tt_idx: p.tensor_tensor(
                            x_sb[:, D * tt_idx:D * (tt_idx + 1)],
                            x_sb[:, D * tt_idx:D * (tt_idx + 1)],
                            u2[:, 2048 * slot + 1024 * tau:2048 * slot + 1024 * (tau + 1)],
                            AL.add),
                            waits=[(s_ap3, ap3_base + 2 * Sc + tau + 1)], inc=s_xadd3)
                for tau in range(2):
                    tt_idx = 8 * H + 2 * Sc + tau
                    stw = ([(s_xadd3, xadd3_base + 6), (s_xadd3d, add_d[tau])]
                           if Sc == 3 else
                           [(s_xadd3, xadd3_base + 2 * Sc + tau + 1)])
                    S.add("sync", lambda s, tt_idx=tt_idx, b=b: s.dma_start(
                        out_hbm[b, 128 * tt_idx:128 * (tt_idx + 1), :],
                        x_sb[:, D * tt_idx:D * (tt_idx + 1)]),
                        waits=stw,
                        inc=q_st[tt_idx // 4], amt=16)
                if b == 0 and Sc % 2 == 1:
                    grp = 2 * H + (Sc - 1) // 2
                    xld_counts[1][grp] = emit_xload(1, grp)
            S.flush(f"b2_{b}_{H}")
            p2.close()
            wds.close()
            fh.close()
        fb.close()

    S.add("sync", None, waits=[(q_st[i], S.cnt[q_st[i]]) for i in range(4)])
    S.flush("fin")
    es.close()
    return nc


_CACHE = {}


def kernel(L0_attn_x, L0_attn_Mq, L0_attn_Mk, L0_attn_Mv,
           L0_ffn_w_gate, L0_ffn_w_up, L0_ffn_w_down):
    x = np.ascontiguousarray(np.asarray(L0_attn_x, dtype=np.float32))
    mq = np.ascontiguousarray(np.asarray(L0_attn_Mq, dtype=np.float32))
    mk = np.ascontiguousarray(np.asarray(L0_attn_Mk, dtype=np.float32))
    mv = np.ascontiguousarray(np.asarray(L0_attn_Mv, dtype=np.float32))
    wg = np.ascontiguousarray(np.asarray(L0_ffn_w_gate, dtype=np.int32))
    wu = np.ascontiguousarray(np.asarray(L0_ffn_w_up, dtype=np.int32))
    wd = np.ascontiguousarray(np.asarray(L0_ffn_w_down, dtype=np.int32))

    if "nc" not in _CACHE:
        _CACHE["nc"] = build()
    nc = _CACHE["nc"]

    in_maps = []
    for c in range(8):
        in_maps.append({
            "x": x[NB * c:NB * (c + 1)],
            "mq": mq, "mk": mk, "mv": mv,
            "wg": wg, "wu": wu, "wd": wd,
        })
    res = run_bass_kernel_spmd(nc, in_maps, core_ids=list(range(8)))
    return np.concatenate([r["out"] for r in res.results], axis=0)


